# version 07
# baseline (speedup 1.0000x reference)
import numpy as np
import concourse.bass as bass
import concourse.mybir as mybir

F32 = mybir.dt.float32
BF16 = mybir.dt.bfloat16
AF = mybir.ActivationFunctionType
ALU = mybir.AluOpType
AX = mybir.AxisListType

TS = 512
NG = 4
NCH = 8
NPAIR = 4
L = 4096
D = 1024
H = 16
I_FFN = 2816
IC_W = 128
N_IC = I_FFN // IC_W

ENGS = ("T", "V", "A", "G", "S")
DRAIN_ENGS = {"V"}


class Buf:
    __slots__ = ("name", "waw", "writers", "readers")

    def __init__(self, name, waw=True):
        self.name = name
        self.waw = waw
        self.writers = []
        self.readers = []


class Step:
    __slots__ = ("idx", "eng", "fn", "dma", "certify", "deps", "needed_cross",
                 "seq", "tag")

    def __init__(self, idx, eng, fn, dma, certify, tag):
        self.idx = idx
        self.eng = eng
        self.fn = fn
        self.dma = dma
        self.certify = certify
        self.deps = None
        self.needed_cross = False
        self.seq = None
        self.tag = tag


class Ctx:
    def __init__(self, nc):
        self.nc = nc
        self.steps = []

    def buf(self, name, waw=True):
        return Buf(name, waw)

    def step(self, eng, fn, reads=(), writes=(), dma=None, certify="last", tag=""):
        s = Step(len(self.steps), eng, fn, dma, certify, tag)
        deps = {}
        for b in reads:
            for w in b.writers:
                deps[w.idx] = w
        for b in writes:
            for r in b.readers:
                deps[r.idx] = r
            if b.waw:
                for w in b.writers:
                    deps[w.idx] = w
        deps.pop(s.idx, None)
        s.deps = [deps[i] for i in sorted(deps)]
        for b in reads:
            if not b.readers or b.readers[-1] is not s:
                b.readers.append(s)
        for b in writes:
            if b.waw:
                b.writers = [s]
                b.readers = []
            else:
                b.writers.append(s)
        self.steps.append(s)
        return s

    NDMA_LANES = 8

    def emit(self):
        nc = self.nc
        for s in self.steps:
            for d in s.deps:
                if d.dma is None and d.eng != s.eng:
                    d.needed_cross = True
        eng_count = {e: 0 for e in ENGS}
        dma_count = {}
        lane_occ = {}
        for s in self.steps:
            if s.dma is not None:
                n = dma_count.get(s.dma, 0)
                dma_count[s.dma] = n + 1
                lane = n % self.NDMA_LANES
                key = (s.dma, lane)
                lane_occ[key] = lane_occ.get(key, 0) + 1
                s.seq = (key, lane_occ[key])
            elif s.needed_cross:
                eng_count[s.eng] += 1
                s.seq = eng_count[s.eng]
        self.final_lane_counts = dict(lane_occ)

        sems = {}
        sem_cms = []
        for e in ENGS:
            if eng_count[e]:
                cm = nc.semaphore("sem_" + e)
                sems[e] = cm.__enter__()
                sem_cms.append(cm)
        for key in sorted(lane_occ):
            cm = nc.semaphore(f"dma_{key[0]}_{key[1]}")
            sems[key] = cm.__enter__()
            sem_cms.append(cm)

        block_cm = nc.Block()
        block = block_cm.__enter__()
        eng_steps = {e: [s for s in self.steps if s.eng == e] for e in ENGS}

        def make_section(eng_name):
            def section(eh):
                waited = {}
                last_idx = -1
                drained_idx = -1
                drained_idx = -1
                for s in eng_steps[eng_name]:
                    need_drain = False
                    waits = []
                    for d in s.deps:
                        if d.dma is not None:
                            key, occ = d.seq
                            val = 16 * occ
                            if waited.get(key, -1) < val:
                                waits.append((key, val))
                        elif d.eng == eng_name:
                            if eng_name in DRAIN_ENGS and d.idx > drained_idx:
                                need_drain = True
                        else:
                            key, val = d.eng, d.seq
                            assert d.seq is not None, (s.tag, d.tag)
                            if waited.get(key, -1) < val:
                                waits.append((key, val))
                    if s.dma is not None:
                        key, occ = s.seq
                        val = 16 * (occ - 1)
                        if val > 0 and waited.get(key, -1) < val:
                            waits.append((key, val))
                    if need_drain:
                        eh.drain()
                        drained_idx = last_idx
                    for key, val in waits:
                        eh.wait_ge(sems[key], val)
                        waited[key] = val
                    last = s.fn(eh)
                    if s.dma is not None:
                        assert last is not None, s.tag
                        key, occ = s.seq
                        last.then_inc(sems[key], 16)
                    elif s.needed_cross:
                        assert last is not None, s.tag
                        last.then_inc(sems[s.eng], 1)
                    last_idx = s.idx
                if eng_name == "S":
                    for key in sorted(self.final_lane_counts):
                        eh.wait_ge(sems[key], 16 * self.final_lane_counts[key])
            return section

        block.tensor(make_section("T"))
        block.vector(make_section("V"))
        block.scalar(make_section("A"))
        block.gpsimd(make_section("G"))
        block.sync(make_section("S"))
        block_cm.__exit__(None, None, None)
        for cm in reversed(sem_cms):
            cm.__exit__(None, None, None)



def host_prep(inputs, core):
    im = {}
    im["x"] = np.ascontiguousarray(np.asarray(inputs["L0_attn_x"])[core])
    im["w_in_t"] = np.ascontiguousarray(np.asarray(inputs["L0_attn_W_in"]).T)
    im["w_dt_t"] = np.ascontiguousarray(np.asarray(inputs["L0_attn_W_dt"]).T)
    im["w_b_t"] = np.ascontiguousarray(np.asarray(inputs["L0_attn_W_B"]).T)
    im["w_c_t"] = np.ascontiguousarray(np.asarray(inputs["L0_attn_W_C"]).T)
    im["w_out_t"] = np.ascontiguousarray(np.asarray(inputs["L0_attn_W_out"]).T)
    im["a_log"] = np.ascontiguousarray(np.asarray(inputs["L0_attn_A_log"]).reshape(16, 1))
    im["b_dt"] = np.ascontiguousarray(np.asarray(inputs["L0_attn_b_dt"]).reshape(16, 1))
    im["w_gate"] = np.ascontiguousarray(np.asarray(inputs["L0_ffn_w_gate"]))
    im["w_up"] = np.ascontiguousarray(np.asarray(inputs["L0_ffn_w_up"]))
    im["w_down"] = np.ascontiguousarray(np.asarray(inputs["L0_ffn_w_down"]))
    im["ident"] = np.eye(128, dtype=np.float32)
    u = np.arange(64)
    im["cum_in"] = np.tile(np.where(u[:, None] <= u[None, :], 1.0, 0.0), (2, 1)).astype(np.float32)
    im["cum_rev"] = np.tile(np.where(u[:, None] > u[None, :], 1.0, 0.0), (2, 1)).astype(np.float32)
    im["m0"] = np.tile(np.where(u[:, None] <= u[None, :], 0.0, -1e30), (2, 1)).astype(np.float32)
    im["ones_row"] = np.ones((1, 128), np.float32)
    return im


DBG_SHAPES = {
    "xn1": [TS, D], "dt_hm": [16, TS], "acs_hm": [16, TS], "Bfm": [8, 128, TS],
    "Cfm": [8, 128, TS], "X": [TS, D], "E0": [128, 1024], "Y": [TS, D],
    "x1": [TS, D], "xq": [TS, D], "wqg": [D, I_FFN], "out_t0": [TS, D],
    "zs": [TS, D],
}


def build_program(NT=8, dbg=(), max_steps=None):
    nc = bass.Bass("TRN2", target_bir_lowering=False, debug=False,
                   detect_race_conditions=False)
    ctx = Ctx(nc)

    dI = lambda name, shape: nc.dram_tensor(name, shape, F32, kind="ExternalInput").ap()
    x_h = dI("x", [L, D])
    w_in_h = dI("w_in_t", [D, 2 * D])
    w_dt_h = dI("w_dt_t", [D, H])
    w_b_h = dI("w_b_t", [D, D])
    w_c_h = dI("w_c_t", [D, D])
    w_out_h = dI("w_out_t", [D, D])
    a_log_h = dI("a_log", [H, 1])
    b_dt_h = dI("b_dt", [H, 1])
    w_gate_h = dI("w_gate", [I_FFN, D])
    w_up_h = dI("w_up", [I_FFN, D])
    w_down_h = dI("w_down", [D, I_FFN])
    ident_h = dI("ident", [128, 128])
    cum_in_h = dI("cum_in", [128, 64])
    cum_rev_h = dI("cum_rev", [128, 64])
    m0_h = dI("m0", [128, 64])
    ones_row_h = dI("ones_row", [1, 128])

    out_h = nc.dram_tensor("out", [L, D], F32, kind="ExternalOutput").ap()
    wqg_h = nc.dram_tensor("wqg", [D, I_FFN], F32, kind="Internal").ap()
    wqu_h = nc.dram_tensor("wqu", [D, I_FFN], F32, kind="Internal").ap()
    wqd_h = nc.dram_tensor("wqd", [I_FFN, D], F32, kind="Internal").ap()

    dbg = set(dbg)
    dbg_h = {}

    def dbg_out(name):
        if name in dbg:
            dbg_h[name] = nc.dram_tensor("dbg_" + name, DBG_SHAPES[name], F32,
                                         kind="ExternalOutput").ap()
            return dbg_h[name]
        return None

    exits = []

    def sb(name, shape, dtype=F32):
        cm = nc.sbuf_tensor(name, shape, dtype)
        t = cm.__enter__()
        exits.append(cm)
        return t

    def ps(name, shape):
        cm = nc.psum_tensor(name, shape, F32)
        t = cm.__enter__()
        exits.append(cm)
        return t

    xres = sb("xres", [128, NG, D])
    buf2 = sb("buf2", [128, NG, D])
    fmA = sb("fmA", [128, 8, TS])
    zs = sb("zs", [128, NG, D])
    Bfm = sb("Bfm", [128, 8, TS])
    Cfm = sb("Cfm", [128, 8, TS])
    Ytm = sb("Ytm", [128, NG, D])
    sc1 = sb("sc1", [128, 1024])
    sc2 = sb("sc2", [128, 1024])
    sc3 = sb("sc3", [128, 1024])
    sc4 = sb("sc4", [128, 1024])
    Rst = sb("Rst", [128, 512])
    ws = [sb(f"ws{i}", [128, 2048]) for i in range(3)]

    dt_hm = sb("dt_hm", [16, TS])
    A_hm = sb("A_hm", [16, TS])
    acs_hm = sb("acs_hm", [16, TS])
    asum = sb("asum", [16, NCH])
    lam_hm = sb("lam_hm", [16, NCH])
    acs_pair = sb("acs_pair", [1, 2048])
    lam_flat = sb("lam_flat", [1, H * NCH])
    dt_tm = sb("dt_tm", [128, NG, H])
    A_tm = sb("A_tm", [128, NG, H])
    acs_tm = sb("acs_tm", [128, NPAIR, H])
    ds_tm = sb("ds_tm", [128, NPAIR, H])
    sdo_tm = sb("sdo_tm", [128, NPAIR, H])
    lam_all = sb("lam_all", [128, 8, 8])
    stats = sb("stats", [128, 6, NG])
    w_dt_s = sb("w_dt_s", [128, 8, H])
    ident_s = sb("ident_s", [128, 128])
    cum_in_s = sb("cum_in_s", [128, 64])
    cum_rev_s = sb("cum_rev_s", [128, 64])
    m0_s = sb("m0_s", [128, 64])
    ones_s = sb("ones_s", [1, 128])
    alog_s = sb("alog_s", [16, 1])
    bdt_s = sb("bdt_s", [16, 1])
    negea = sb("negea", [16, 1])
    negea_row = sb("negea_row", [1, 16])
    negea_tile = sb("negea_tile", [128, 16])
    actc = sb("actc", [128, 4])
    zc16 = sb("zc16", [16, 64])
    qm = sb("qm", [128, 176])
    qs = sb("qs", [128, 176])
    qi2 = sb("qi2", [128, 176])
    qsh = sb("qsh", [128, 176])
    qt6 = sb("qt6", [128, 176])
    qa_t = sb("qa_t", [128, D])
    qu_t = sb("qu_t", [128, D])
    qw1 = sb("qw1", [128, D], BF16)
    qw2 = sb("qw2", [128, D], BF16)
    qr1 = sb("qr1", [128, D], BF16)
    qr2 = sb("qr2", [128, D], BF16)
    gate_s = sb("gate_s", [128, 2 * IC_W])
    h_s = sb("h_s", [128, 2 * IC_W])
    hq_s = sb("hq_s", [128, 2 * IC_W])
    hqfm = sb("hqfm", [128, 2, 128])

    PA = ps("PA", [128, 1024])
    PB = ps("PB", [128, 1024])
    PC = ps("PC", [128, 1024])
    PD = ps("PD", [128, 1024])

    B = ctx.buf
    b_xres = B("xres")
    b_buf2 = B("buf2")
    b_fmA = B("fmA")
    b_zs = B("zs")
    b_Bfm = B("Bfm")
    b_Cfm = B("Cfm")
    b_Ytm = B("Ytm")
    b_sc1 = B("sc1")
    b_sc2 = B("sc2")
    b_sc3 = B("sc3")
    b_sc4 = B("sc4")
    b_R = B("Rst")
    b_wsh = [[B(f"ws{i}h{j}") for j in range(2)] for i in range(3)]
    b_dt = B("dt")
    b_acs = B("acs")
    b_lam = B("lam")
    b_acsflat = B("acsflat")
    b_lamflat = B("lamflat")
    b_dttm = B("dttm")
    b_acstm = B("acstm")
    b_dssdo = B("dssdo")
    b_lampair = B("lampair")
    b_stats = B("stats")
    b_consts = B("consts")
    b_q = B("quant")
    b_ffn = B("ffn")
    b_PA = [B("PA0"), B("PA1")]
    b_PB = [B("PB0"), B("PB1")]
    b_PC = [B("PC0"), B("PC1")]
    b_PD = [B("PD0"), B("PD1")]
    b_wqg = B("wqg", waw=False)
    b_wqu = B("wqu", waw=False)
    b_wqd = B("wqd", waw=False)
    b_out = B("out_dram", waw=False)
    b_dbg = B("dbg", waw=False)

    psum_halves = {id(PA): b_PA, id(PB): b_PB, id(PC): b_PC, id(PD): b_PD}

    def pbufs(pt, c0, c1):
        hs = psum_halves[id(pt)]
        out = []
        if c0 < 512:
            out.append(hs[0])
        if c1 > 512:
            out.append(hs[1])
        return out

    st = ctx.step

    def dma(dst_ap, src_ap):
        def fn(eh):
            return eh.dma_start(out=dst_ap, in_=src_ap)
        return fn

    for dst, src in [(ident_s[:, :], ident_h), (cum_in_s[:, :], cum_in_h),
                     (cum_rev_s[:, :], cum_rev_h), (m0_s[:, :], m0_h),
                     (ones_s[:, :], ones_row_h), (alog_s[:, :], a_log_h),
                     (bdt_s[:, :], b_dt_h)]:
        st("S", dma(dst, src), reads=[], writes=[b_consts], dma="dS", tag="ldconst")
    st("S", dma(w_dt_s[:, :, :], w_dt_h.rearrange("(k p) h -> p k h", p=128)),
       reads=[], writes=[b_consts], dma="dS", tag="ldwdt")
    st("V", lambda eh: eh.memset(Rst[:, :], 0.0), reads=[], writes=[b_R], tag="zeroR")
    st("V", lambda eh: eh.memset(actc[:, 0:1], 128.0), reads=[], writes=[b_consts], tag="c128")
    st("V", lambda eh: eh.memset(zc16[:, :], 0.0), reads=[], writes=[b_consts], tag="zc16")
    st("V", lambda eh: eh.memset(actc[:, 1:2], float(np.float32(1e-5))), reads=[], writes=[b_consts], tag="ceps")
    st("V", lambda eh: eh.memset(actc[:, 2:3], 0.5), reads=[], writes=[b_consts], tag="chalf")
    st("V", lambda eh: eh.memset(actc[:, 3:4], -1.0), reads=[], writes=[b_consts], tag="cneg1")
    st("A", lambda eh: eh.activation(negea[:, :], alog_s[:, :], AF.Exp),
       reads=[b_consts], writes=[b_consts], tag="negea_exp")
    st("V", lambda eh: eh.tensor_scalar(negea[:, :], negea[:, :], -1.0, None, ALU.mult),
       reads=[b_consts], writes=[b_consts], tag="negea_neg")
    st("T", lambda eh: eh.transpose(PD[0:1, 0:16], negea[:, :], ident_s[0:16, 0:16]),
       reads=[b_consts], writes=[b_PD[0]], tag="negea_T")
    st("V", lambda eh: eh.tensor_copy(negea_row[:, :], PD[0:1, 0:16]),
       reads=[b_PD[0]], writes=[b_consts], tag="negea_row")
    st("T", lambda eh: eh.matmul(PD[0:128, 16:32], ones_s[0:1, :], negea_row[0:1, :],
                                 start=True, stop=True),
       reads=[b_consts], writes=[b_PD[0]], tag="negea_bc")
    st("V", lambda eh: eh.tensor_copy(negea_tile[:, :], PD[0:128, 16:32]),
       reads=[b_PD[0]], writes=[b_consts], tag="negea_tile")

    def emit_quant(src_ap, dst_ap, W, rbufs, wbufs, tag):
        nblk = W // 16
        a3 = src_ap.rearrange("p (g b) -> p g b", b=16)

        def bc(tile):
            return tile[:, 0:nblk].unsqueeze(2).broadcast_to([128, nblk, 16])

        st("V", lambda eh: eh.tensor_reduce(qm[:, 0:nblk], a3, AX.X, ALU.max,
                                            apply_absolute_value=True),
           reads=list(rbufs), writes=[b_q], tag=tag + ".m")
        st("V", lambda eh: eh.tensor_scalar(qs[:, 0:nblk], qm[:, 0:nblk],
                                            float(np.float32(1.0 / 6.0)), float(np.float32(1e-12)),
                                            ALU.mult, ALU.max),
           reads=[b_q], writes=[b_q], tag=tag + ".s")
        st("V", lambda eh: eh.reciprocal(qi2[:, 0:nblk], qs[:, 0:nblk]),
           reads=[b_q], writes=[b_q], tag=tag + ".r")
        st("V", lambda eh: eh.tensor_scalar(qi2[:, 0:nblk], qi2[:, 0:nblk], 2.0, None,
                                            ALU.mult),
           reads=[b_q], writes=[b_q], tag=tag + ".r2")
        st("V", lambda eh: eh.tensor_scalar(qsh[:, 0:nblk], qs[:, 0:nblk], 0.5, None,
                                            ALU.mult),
           reads=[b_q], writes=[b_q], tag=tag + ".sh")
        st("V", lambda eh: eh.tensor_scalar(qt6[:, 0:nblk], qs[:, 0:nblk], 5.0, None,
                                            ALU.mult),
           reads=[b_q], writes=[b_q], tag=tag + ".t6")
        st("A", lambda eh: eh.activation(qa_t[:, 0:W], src_ap, AF.Abs),
           reads=list(rbufs), writes=[b_q], tag=tag + ".abs")
        st("V", lambda eh: eh.tensor_tensor(
            qu_t[:, 0:W].rearrange("p (g b) -> p g b", b=16),
            qa_t[:, 0:W].rearrange("p (g b) -> p g b", b=16), bc(qi2), ALU.mult),
           reads=[b_q], writes=[b_q], tag=tag + ".u")
        st("A", lambda eh: eh.activation(qw1[:, 0:W], qu_t[:, 0:W], AF.Identity,
                                         bias=actc[:, 0:1]),
           reads=[b_q, b_consts], writes=[b_q], tag=tag + ".w1")
        st("A", lambda eh: eh.activation(qw2[:, 0:W], qu_t[:, 0:W], AF.Identity,
                                         bias=actc[:, 0:1], scale=actc[:, 2:3]),
           reads=[b_q, b_consts], writes=[b_q], tag=tag + ".w2")
        st("V", lambda eh: eh.tensor_scalar(qr1[:, 0:W], qw1[:, 0:W], 132.0, 128.0,
                                            ALU.min, ALU.subtract),
           reads=[b_q], writes=[b_q], tag=tag + ".qa")
        st("V", lambda eh: eh.tensor_scalar(qr2[:, 0:W], qw2[:, 0:W], 132.0, 130.0,
                                            ALU.min, ALU.subtract),
           reads=[b_q], writes=[b_q], tag=tag + ".qb")
        st("V", lambda eh: eh.tensor_tensor(
            qu_t[:, 0:W].rearrange("p (g b) -> p g b", b=16),
            qa_t[:, 0:W].rearrange("p (g b) -> p g b", b=16), bc(qt6), ALU.is_gt),
           reads=[b_q], writes=[b_q], tag=tag + ".c6")
        st("V", lambda eh: eh.tensor_scalar(qr2[:, 0:W], qr2[:, 0:W], 0.0, 2.0,
                                            ALU.max, ALU.mult),
           reads=[b_q], writes=[b_q], tag=tag + ".qb2")
        st("V", lambda eh: eh.tensor_tensor(qr2[:, 0:W], qr2[:, 0:W], qr1[:, 0:W],
                                            ALU.add),
           reads=[b_q], writes=[b_q], tag=tag + ".t8")
        st("V", lambda eh: eh.scalar_tensor_tensor(qr2[:, 0:W], qu_t[:, 0:W], 4.0,
                                                   qr2[:, 0:W], ALU.mult, ALU.add),
           reads=[b_q], writes=[b_q], tag=tag + ".q2")
        st("A", lambda eh: eh.activation(qa_t[:, 0:W], src_ap, AF.Sign),
           reads=list(rbufs) + [b_q], writes=[b_q], tag=tag + ".sg")
        st("V", lambda eh: eh.tensor_tensor(
            qa_t[:, 0:W].rearrange("p (g b) -> p g b", b=16),
            qa_t[:, 0:W].rearrange("p (g b) -> p g b", b=16), bc(qsh), ALU.mult),
           reads=[b_q], writes=[b_q], tag=tag + ".sgs")
        st("V", lambda eh: eh.tensor_tensor(dst_ap, qr2[:, 0:W], qa_t[:, 0:W], ALU.mult),
           reads=[b_q], writes=list(wbufs), tag=tag + ".xq")

    def prep_wq(w_h, wq_h, nrows, ncols, b_wq, tag):
        ntile = nrows // 128
        nkb = ncols // 128
        assert ncols <= 2048 or True
        for it in range(ntile):
            if ncols <= 2048:
                wsl, bsl = ws[it % 2], b_wsh[it % 2]
                st("G", dma(wsl[:, 0:ncols], w_h[it * 128:(it + 1) * 128, :]),
                   reads=bsl, writes=bsl, dma="dG", tag=f"{tag}.ld{it}")
            else:
                wsl, bsl = None, b_wsh[0] + b_wsh[1]
                st("G", dma(ws[0][:, 0:2048], w_h[it * 128:(it + 1) * 128, 0:2048]),
                   reads=b_wsh[0], writes=b_wsh[0], dma="dG", tag=f"{tag}.ld{it}a")
                st("G", dma(ws[1][:, 0:ncols - 2048],
                            w_h[it * 128:(it + 1) * 128, 2048:ncols]),
                   reads=b_wsh[1], writes=b_wsh[1], dma="dG", tag=f"{tag}.ld{it}b")
            def wcol(c0):
                if ncols <= 2048:
                    return ws[it % 2], c0
                return (ws[0], c0) if c0 < 2048 else (ws[1], c0 - 2048)
            for c0 in range(0, ncols, 1024):
                w = min(1024, ncols - c0)
                wt, cc = wcol(c0)
                emit_quant(wt[:, cc:cc + w], wt[:, cc:cc + w], w, bsl, bsl,
                           f"{tag}.q{it}.{c0}")
            for kb in range(nkb):
                pslot = PB if (kb % 2 == 0) else PC
                pb = psum_halves[id(pslot)][0]
                wt, cc = wcol(kb * 128)
                st("T", lambda eh, wt=wt, cc=cc, pslot=pslot: eh.transpose(
                    pslot[:, 0:128], wt[:, cc:cc + 128], ident_s[:, :]),
                   reads=bsl + [b_consts], writes=[pb], tag=f"{tag}.T{it}.{kb}")
                st("V", lambda eh, kb=kb, pslot=pslot: eh.tensor_copy(
                    ws[2][:, (kb % 4) * 128:(kb % 4) * 128 + 128], pslot[:, 0:128]),
                   reads=[pb], writes=b_wsh[2], tag=f"{tag}.evT{it}.{kb}")
                if kb % 4 == 3 or kb == nkb - 1:
                    kb0 = (kb // 4) * 4
                    nkb_i = kb - kb0 + 1
                    st("G", dma(wq_h[kb0 * 128:kb0 * 128 + nkb_i * 128,
                                     it * 128:(it + 1) * 128].rearrange(
                                    "(kb p) t -> p kb t", p=128),
                                ws[2][:, 0:nkb_i * 128].rearrange(
                                    "p (kb t) -> p kb t", t=128)),
                       reads=b_wsh[2], writes=[b_wq], dma="dG",
                       tag=f"{tag}.st{it}.{kb0}")

    prep_wq(w_gate_h, wqg_h, I_FFN, D, b_wqg, "pg")
    prep_wq(w_up_h, wqu_h, I_FFN, D, b_wqu, "pu")
    prep_wq(w_down_h, wqd_h, D, I_FFN, b_wqd, "pd")

    d = dbg_out("wqg")
    if d is not None:
        for r in range(D // 128):
            st("G", dma(d[r * 128:(r + 1) * 128, :], wqg_h[r * 128:(r + 1) * 128, :]),
               reads=[b_wqg], writes=[b_dbg], dma="dG", tag=f"dbgwqg{r}")

    def dbg_dump_tm(name, buf, rbuf):
        d = dbg_out(name)
        if d is not None:
            st("G", dma(d.rearrange("(g p) d -> p g d", p=128), buf[:, :, :]),
               reads=[rbuf], writes=[b_dbg], dma="dG", tag="dbg" + name)

    for t in range(NT):
        tb = t * TS
        st("S", dma(xres[:, :, :], x_h[tb:tb + TS, :].rearrange("(g p) d -> p g d", p=128)),
           reads=[b_xres], writes=[b_xres], dma="dS", tag=f"t{t}.ldx")

        def layernorm(src_buf, src_b, dst_buf, dst_b, tag):
            for g in range(NG):
                st("V", lambda eh, g=g: eh.reduce_sum(stats[:, 0, g:g + 1],
                                                      src_buf[:, g, :], AX.X),
                   reads=[src_b], writes=[b_stats], tag=f"{tag}.sum{g}")
                st("A", lambda eh, g=g: eh.activation(dst_buf[:, g, :], src_buf[:, g, :],
                                                      AF.Square,
                                                      accum_out=stats[:, 1, g:g + 1]),
                   reads=[src_b], writes=[dst_b, b_stats], tag=f"{tag}.sq{g}")
            st("V", lambda eh: eh.tensor_scalar(stats[:, 2, :], stats[:, 0, :],
                                                float(np.float32(1.0 / D)), None, ALU.mult),
               reads=[b_stats], writes=[b_stats], tag=f"{tag}.mean")
            st("V", lambda eh: eh.tensor_tensor(stats[:, 3, :], stats[:, 2, :],
                                                stats[:, 2, :], ALU.mult),
               reads=[b_stats], writes=[b_stats], tag=f"{tag}.msq")
            st("V", lambda eh: eh.scalar_tensor_tensor(stats[:, 4, :], stats[:, 1, :],
                                                       float(np.float32(1.0 / D)),
                                                       stats[:, 3, :],
                                                       ALU.mult, ALU.subtract),
               reads=[b_stats], writes=[b_stats], tag=f"{tag}.var")
            st("A", lambda eh: eh.activation(stats[:, 5, :], stats[:, 4, :], AF.Sqrt,
                                             bias=actc[:, 1:2]),
               reads=[b_stats, b_consts], writes=[b_stats], tag=f"{tag}.sd")
            st("V", lambda eh: eh.reciprocal(stats[:, 5, :], stats[:, 5, :]),
               reads=[b_stats], writes=[b_stats], tag=f"{tag}.rstd")
            for g in range(NG):
                st("V", lambda eh, g=g: eh.scalar_tensor_tensor(
                    dst_buf[:, g, :], src_buf[:, g, :], stats[:, 2, g:g + 1],
                    stats[:, 5, g:g + 1].broadcast_to([128, D]),
                    ALU.subtract, ALU.mult),
                   reads=[src_b, b_stats], writes=[dst_b], tag=f"{tag}.norm{g}")

        layernorm(xres, b_xres, buf2, b_buf2, f"t{t}.ln1")
        if t == 0:
            dbg_dump_tm("xn1", buf2, b_buf2)

        for g in range(NG):
            for k in range(8):
                pslot = PA if ((g * 8 + k) % 2 == 0) else PB
                pb = psum_halves[id(pslot)][0]
                st("T", lambda eh, g=g, k=k, pslot=pslot: eh.transpose(
                    pslot[:, 0:128], buf2[:, g, k * 128:(k + 1) * 128], ident_s[:, :]),
                   reads=[b_buf2, b_consts], writes=[pb], tag=f"t{t}.xT{g}.{k}")
                st("V", lambda eh, g=g, k=k, pslot=pslot: eh.tensor_copy(
                    fmA[:, k, g * 128:(g + 1) * 128], pslot[:, 0:128]),
                   reads=[pb], writes=[b_fmA], tag=f"t{t}.xTe{g}.{k}")

        def mm_dt(eh):
            ins = None
            for k in range(8):
                ins = eh.matmul(PD[0:16, 0:TS], w_dt_s[:, k, :], fmA[:, k, 0:TS],
                                start=(k == 0), stop=(k == 7))
            return ins
        st("T", mm_dt, reads=[b_consts, b_fmA], writes=[b_PD[0]], tag=f"t{t}.mmdt")
        st("A", lambda eh: eh.activation(dt_hm[:, :], PD[0:16, 0:TS], AF.Identity,
                                         bias=bdt_s[:, 0:1]),
           reads=[b_PD[0], b_consts], writes=[b_dt], tag=f"t{t}.dtz")
        st("A", lambda eh: eh.activation(A_hm[:, :], dt_hm[:, :], AF.Abs),
           reads=[b_dt], writes=[b_dt], tag=f"t{t}.dtabs")
        st("A", lambda eh: eh.activation(A_hm[:, :], A_hm[:, :], AF.Exp,
                                         scale=actc[0:16, 3:4]),
           reads=[b_dt, b_consts], writes=[b_dt], tag=f"t{t}.dtexp")
        st("A", lambda eh: eh.activation(A_hm[:, :], A_hm[:, :], AF.Ln, bias=1.0),
           reads=[b_dt], writes=[b_dt], tag=f"t{t}.dtln")
        st("V", lambda eh: eh.scalar_tensor_tensor(dt_hm[:, :], dt_hm[:, :], 0.0,
                                                   A_hm[:, :], ALU.max, ALU.add),
           reads=[b_dt], writes=[b_dt], tag=f"t{t}.dtsp")
        if t == 0:
            d = dbg_out("dt_hm")
            if d is not None:
                st("G", dma(d, dt_hm[:, :]), reads=[b_dt], writes=[b_dbg], dma="dG",
                   tag="dbgdt")
        st("V", lambda eh: eh.tensor_scalar(A_hm[:, :], dt_hm[:, :], negea[:, 0:1],
                                            None, ALU.mult),
           reads=[b_dt, b_consts], writes=[b_dt], tag=f"t{t}.ahm")
        st("V", lambda eh: eh.reduce_sum(asum[:, :],
                                         A_hm[:, :].rearrange("p (c l) -> p c l", l=64),
                                         AX.X),
           reads=[b_dt], writes=[b_lam], tag=f"t{t}.asum")
        st("A", lambda eh: eh.activation(lam_hm[:, :], asum[:, :], AF.Exp),
           reads=[b_lam], writes=[b_lam], tag=f"t{t}.lam")
        for c in range(NCH):
            st("V", lambda eh, c=c: eh.tensor_tensor_scan(
                acs_hm[:, c * 64:(c + 1) * 64], zc16[:, :],
                A_hm[:, c * 64:(c + 1) * 64], 0.0, ALU.add, ALU.add),
               reads=[b_dt, b_consts], writes=[b_acs], tag=f"t{t}.scan{c}")
        if t == 0:
            d = dbg_out("acs_hm")
            if d is not None:
                st("G", dma(d, acs_hm[:, :]), reads=[b_acs], writes=[b_dbg], dma="dG",
                   tag="dbgacs")
        st("G", dma(lam_flat[0:1, :].rearrange("o (h c) -> o h c", h=16),
                    lam_hm[:, :]),
           reads=[b_lam], writes=[b_lamflat], dma="dG", tag=f"t{t}.lamflat")
        st("T", lambda eh: eh.matmul(
            PD[0:64, 544:608], ones_s[0:1, 0:64],
            lam_flat[0:1, :].rearrange("o (j two c) -> o two c j", two=2, c=8)[0:1, 0, :, :],
            start=True, stop=True),
           reads=[b_lamflat, b_consts], writes=[b_PD[1]], tag=f"t{t}.lamA")
        st("T", lambda eh: eh.matmul(
            PD[64:128, 544:608], ones_s[0:1, 0:64],
            lam_flat[0:1, :].rearrange("o (j two c) -> o two c j", two=2, c=8)[0:1, 1, :, :],
            start=True, stop=True, tile_position=(0, 64)),
           reads=[b_lamflat, b_consts], writes=[b_PD[1]], tag=f"t{t}.lamB")
        st("V", lambda eh: eh.tensor_copy(lam_all[:, :, :].rearrange("p a b -> p (a b)"),
                                          PD[:, 544:608]),
           reads=[b_PD[1]], writes=[b_lampair], tag=f"t{t}.lamE")

        for g in range(NG):
            st("T", lambda eh, g=g: eh.transpose(PD[0:128, 512:528],
                                                 dt_hm[:, g * 128:(g + 1) * 128],
                                                 ident_s[0:16, 0:16]),
               reads=[b_dt, b_consts], writes=[b_PD[1]], tag=f"t{t}.dtT{g}")
            st("V", lambda eh, g=g: eh.tensor_copy(dt_tm[:, g, :], PD[0:128, 512:528]),
               reads=[b_PD[1]], writes=[b_dttm], tag=f"t{t}.dtTe{g}")
            st("V", lambda eh, g=g: eh.tensor_tensor(A_tm[:, g, :], dt_tm[:, g, :],
                                                     negea_tile[:, :], ALU.mult),
               reads=[b_dttm, b_consts], writes=[b_dttm], tag=f"t{t}.atm{g}")

        for c in range(NCH):
            g, par = c // 2, c % 2
            r0 = par * 64
            p = c // 2
            st("T", lambda eh, g=g, r0=r0: eh.matmul(
                PD[r0:r0 + 64, 512:528], cum_in_s[r0:r0 + 64, :],
                A_tm[r0:r0 + 64, g, :], start=True, stop=True,
                tile_position=(r0, r0)),
               reads=[b_dttm, b_consts], writes=[b_PD[1]], tag=f"t{t}.acsmm{c}")
            st("T", lambda eh, g=g, r0=r0: eh.matmul(
                PD[r0:r0 + 64, 528:544], cum_rev_s[r0:r0 + 64, :],
                A_tm[r0:r0 + 64, g, :], start=True, stop=True,
                tile_position=(r0, r0)),
               reads=[b_dttm, b_consts], writes=[b_PD[1]], tag=f"t{t}.rcmm{c}")
            st("V", lambda eh, p=p, r0=r0: eh.tensor_copy(
                acs_tm[r0:r0 + 64, p, :], PD[r0:r0 + 64, 512:528]),
               reads=[b_PD[1]], writes=[b_acstm], tag=f"t{t}.acstm{c}")
            st("A", lambda eh, p=p, r0=r0: eh.activation(
                sdo_tm[r0:r0 + 64, p, :], PD[r0:r0 + 64, 512:528], AF.Exp),
               reads=[b_PD[1]], writes=[b_dssdo], tag=f"t{t}.sdo{c}")
            st("A", lambda eh, p=p, r0=r0: eh.activation(
                ds_tm[r0:r0 + 64, p, :], PD[r0:r0 + 64, 528:544], AF.Exp),
               reads=[b_PD[1]], writes=[b_dssdo], tag=f"t{t}.ds{c}")

        for jc in range(8):
            wslot, bslot = ws[jc % 2], [b_wsh[jc % 2][0], b_wsh[jc % 2][1]]
            st("S", dma(wslot[:, :].rearrange("p (k j) -> p k j", j=256),
                        w_in_h[:, jc * 256:(jc + 1) * 256].rearrange(
                            "(k p) j -> p k j", p=128)),
               reads=bslot, writes=bslot, dma="dS", tag=f"t{t}.ldwin{jc}")
            for g in range(NG):
                pslot = [PA, PA, PB, PB][g]
                pc0 = 512 * (g % 2) + 256 * (jc % 2)
                pb = pbufs(pslot, pc0, pc0 + 256)

                def mmxz(eh, g=g, wslot=wslot, pslot=pslot, pc0=pc0):
                    ins = None
                    for k in range(8):
                        ins = eh.matmul(pslot[:, pc0:pc0 + 256],
                                        fmA[:, k, g * 128:(g + 1) * 128],
                                        wslot[:, k * 256:(k + 1) * 256],
                                        start=(k == 0), stop=(k == 7))
                    return ins
                st("T", mmxz, reads=[b_fmA] + bslot, writes=pb, tag=f"t{t}.mmxz{jc}.{g}")
                if jc < 4:
                    st("V", lambda eh, g=g, jc=jc, pslot=pslot, pc0=pc0: eh.tensor_copy(
                        buf2[:, g, jc * 256:(jc + 1) * 256], pslot[:, pc0:pc0 + 256]),
                       reads=pb, writes=[b_buf2], tag=f"t{t}.evxin{jc}.{g}")
                else:
                    st("A", lambda eh, g=g, jc=jc, pslot=pslot, pc0=pc0: eh.activation(
                        zs[:, g, (jc - 4) * 256:(jc - 3) * 256],
                        pslot[:, pc0:pc0 + 256], AF.Silu),
                       reads=pb, writes=[b_zs], tag=f"t{t}.evz{jc}.{g}")

        for g in range(NG):
            st("V", lambda eh, g=g: eh.tensor_tensor(
                buf2[:, g, :].rearrange("p (h f) -> p h f", f=64),
                buf2[:, g, :].rearrange("p (h f) -> p h f", f=64),
                dt_tm[:, g, :].unsqueeze(2).broadcast_to([128, H, 64]), ALU.mult),
               reads=[b_buf2, b_dttm], writes=[b_buf2], tag=f"t{t}.X{g}")
        if t == 0:
            dbg_dump_tm("X", buf2, b_buf2)

        for (w_hh, dstbuf, dstb, nm) in [(w_b_h, Bfm, b_Bfm, "B"), (w_c_h, Cfm, b_Cfm, "C")]:
            for q in range(4):
                wslot, bslot = ws[q % 2], [b_wsh[q % 2][0], b_wsh[q % 2][1]]
                st("S", dma(wslot[:, :].rearrange("p (k j) -> p k j", j=256),
                            w_hh[:, q * 256:(q + 1) * 256].rearrange(
                                "(k p) j -> p k j", p=128)),
                   reads=bslot, writes=bslot, dma="dS", tag=f"t{t}.ldw{nm}{q}")
                for j2 in range(2):
                    nk = q * 2 + j2
                    pslot = [PA, PB][j2]
                    pc0 = 512 * (q % 2)
                    pb = pbufs(pslot, pc0, pc0 + 512)

                    def mmbc(eh, wslot=wslot, j2=j2, pslot=pslot, pc0=pc0):
                        ins = None
                        for k in range(8):
                            ins = eh.matmul(
                                pslot[:, pc0:pc0 + 512],
                                wslot[:, k * 256 + j2 * 128:k * 256 + j2 * 128 + 128],
                                fmA[:, k, 0:TS], start=(k == 0), stop=(k == 7))
                        return ins
                    st("T", mmbc, reads=[b_fmA] + bslot, writes=pb, tag=f"t{t}.mm{nm}{nk}")
                    st("V", lambda eh, nk=nk, pslot=pslot, pc0=pc0, dstbuf=dstbuf:
                       eh.tensor_copy(dstbuf[:, nk, :], pslot[:, pc0:pc0 + 512]),
                       reads=pb, writes=[dstb], tag=f"t{t}.ev{nm}{nk}")

        if t == 0:
            for nm, bb, br in [("Bfm", Bfm, b_Bfm), ("Cfm", Cfm, b_Cfm)]:
                d = dbg_out(nm)
                if d is not None:
                    st("G", dma(d.rearrange("k p tt -> p k tt"), bb[:, :, :]),
                       reads=[br], writes=[b_dbg], dma="dG", tag="dbg" + nm)
            dbg_dump_tm("zs", zs, b_zs)

        for p in range(NPAIR):
            g = p
            st("G", dma(acs_pair[0:1, :].rearrange("o (h l) -> o h l", l=128),
                        acs_hm[:, p * 128:(p + 1) * 128]),
               reads=[b_acs], writes=[b_acsflat], dma="dG", tag=f"t{t}.p{p}.acsst")
            for half64 in range(2):
                c = 2 * p + half64
                r0 = half64 * 64
                for par in range(2):
                    def mmD(eh, r0=r0, par=par, half64=half64):
                        rhs = acs_pair[0:1, :].rearrange(
                            "o (j par l) -> o par j l", par=2, l=128)[
                            0:1, par, :, half64 * 64:half64 * 64 + 64]
                        return eh.matmul(PA[r0:r0 + 64, par * 512:(par + 1) * 512],
                                         ones_s[0:1, 0:64], rhs, start=True, stop=True,
                                         tile_position=(0, r0))
                    st("T", mmD, reads=[b_acsflat, b_consts],
                       writes=pbufs(PA, par * 512, (par + 1) * 512),
                       tag=f"t{t}.p{p}.D{half64}.{par}")
            for par in range(2):
                st("V", lambda eh, p=p, par=par: eh.tensor_tensor(
                    sc1[:, par * 512:(par + 1) * 512].rearrange("q (j l) -> q j l", l=64),
                    PA[:, par * 512:(par + 1) * 512].rearrange("q (j l) -> q j l", l=64),
                    acs_tm[:, p, :].rearrange("q (j par2) -> q par2 j", par2=2)[
                        :, par, :].unsqueeze(2).broadcast_to([128, 8, 64]), ALU.subtract),
                   reads=[b_PA[par], b_acstm], writes=[b_sc1], tag=f"t{t}.p{p}.Dsub{par}")
            st("V", lambda eh: eh.tensor_tensor(
                sc1[:, :].rearrange("q (h l) -> q h l", l=64),
                sc1[:, :].rearrange("q (h l) -> q h l", l=64),
                m0_s[:, :].unsqueeze(1).broadcast_to([128, H, 64]), ALU.add),
               reads=[b_sc1, b_consts], writes=[b_sc1], tag=f"t{t}.p{p}.Dm0")
            st("A", lambda eh: eh.activation(sc1[:, :], sc1[:, :], AF.Exp),
               reads=[b_sc1], writes=[b_sc1], tag=f"t{t}.p{p}.Eexp")
            if p == 0 and t == 0:
                d = dbg_out("E0")
                if d is not None:
                    st("G", dma(d, sc1[:, :]), reads=[b_sc1], writes=[b_dbg],
                       dma="dG", tag="dbgE0")

            for half64 in range(2):
                c = 2 * p + half64
                r0 = half64 * 64
                for h in range(H):
                    hb = (h % 2) * 64
                    pc = (h % 2) * 512 + (h // 2) * 64

                    def mmGT(eh, c=c, r0=r0, h=h, hb=hb, pc=pc):
                        return eh.matmul(
                            PB[r0:r0 + 64, pc:pc + 64],
                            Bfm[hb:hb + 64, h // 2, c * 64:(c + 1) * 64],
                            Cfm[hb:hb + 64, h // 2, c * 64:(c + 1) * 64],
                            start=True, stop=True, tile_position=(hb, r0))
                    st("T", mmGT, reads=[b_Bfm, b_Cfm],
                       writes=pbufs(PB, pc, pc + 64),
                       tag=f"t{t}.p{p}.GT{c}.{h}")
            st("V", lambda eh: eh.tensor_tensor(sc2[:, :], sc1[:, :], PB[:, :], ALU.mult),
               reads=[b_sc1, b_PB[0], b_PB[1]], writes=[b_sc2], tag=f"t{t}.p{p}.EG")

            for nk in range(8):
                st("T", lambda eh, nk=nk, p=p: eh.transpose(
                    PC[:, 0:128], Bfm[:, nk, p * 128:(p + 1) * 128], ident_s[:, :]),
                   reads=[b_Bfm, b_consts], writes=[b_PC[0]], tag=f"t{t}.p{p}.BT{nk}")
                st("V", lambda eh, nk=nk: eh.tensor_copy(
                    sc4[:, nk * 128:(nk + 1) * 128], PC[:, 0:128]),
                   reads=[b_PC[0]], writes=[b_sc4], tag=f"t{t}.p{p}.BTe{nk}")
            st("V", lambda eh, p=p: eh.tensor_tensor(
                sc3[:, :].rearrange("q (h f) -> q h f", f=64),
                sc4[:, :].rearrange("q (h f) -> q h f", f=64),
                ds_tm[:, p, :].unsqueeze(2).broadcast_to([128, H, 64]), ALU.mult),
               reads=[b_sc4, b_dssdo], writes=[b_sc3], tag=f"t{t}.p{p}.btil")

            for half64 in range(2):
                r0 = half64 * 64
                for h in range(H):
                    hb = (h % 2) * 64

                    def mmS(eh, r0=r0, h=h, hb=hb, g=g, half64=half64):
                        return eh.matmul(
                            PA[hb:hb + 64,
                               half64 * 512 + (h // 2) * 64:half64 * 512 + (h // 2) * 64 + 64],
                            sc3[r0:r0 + 64, h * 64:(h + 1) * 64],
                            buf2[r0:r0 + 64, g, h * 64:(h + 1) * 64],
                            start=True, stop=True, tile_position=(r0, hb))
                    st("T", mmS, reads=[b_sc3, b_buf2],
                       writes=pbufs(PA, half64 * 512, half64 * 512 + 512),
                       tag=f"t{t}.p{p}.S{half64}.{h}")

            for half64 in range(2):
                r0 = half64 * 64
                for h in range(H):
                    pc = (h % 2) * 512 + (h // 2) * 64

                    def mmYd(eh, r0=r0, h=h, g=g, pc=pc):
                        return eh.matmul(
                            PC[r0:r0 + 64, h * 64:(h + 1) * 64],
                            sc2[r0:r0 + 64, pc:pc + 64],
                            buf2[r0:r0 + 64, g, h * 64:(h + 1) * 64],
                            start=True, stop=True, tile_position=(r0, r0))
                    st("T", mmYd, reads=[b_sc2, b_buf2],
                       writes=pbufs(PC, h * 64, (h + 1) * 64),
                       tag=f"t{t}.p{p}.Yd{half64}.{h}")

            for half64 in range(2):
                c = 2 * p + half64
                r0 = half64 * 64
                for h in range(H):
                    hb = (h % 2) * 64
                    pc = (h % 2) * 512 + (h // 2) * 64

                    def mmYo(eh, c=c, r0=r0, h=h, hb=hb, pc=pc):
                        return eh.matmul(
                            PD[r0:r0 + 64, pc:pc + 64],
                            Cfm[hb:hb + 64, h // 2, c * 64:(c + 1) * 64],
                            Rst[hb:hb + 64, (h // 2) * 64:(h // 2) * 64 + 64],
                            start=True, stop=True, tile_position=(hb, r0))
                    st("T", mmYo, reads=[b_Cfm, b_R],
                       writes=pbufs(PD, pc, pc + 64),
                       tag=f"t{t}.p{p}.Yo{c}.{h}")
                cc = 2 * p + half64
                for j in range(8):
                    def rupd(eh, j=j, half64=half64, cc=cc):
                        return eh.scalar_tensor_tensor(
                            Rst[:, j * 64:(j + 1) * 64], Rst[:, j * 64:(j + 1) * 64],
                            lam_all[:, cc, j:j + 1],
                            PA[:, half64 * 512 + j * 64:half64 * 512 + j * 64 + 64],
                            ALU.mult, ALU.add)
                    st("V", rupd, reads=[b_R, b_lampair, b_PA[half64]], writes=[b_R],
                       tag=f"t{t}.p{p}.Rup{half64}.{j}")

            st("V", lambda eh, g=g: eh.tensor_copy(Ytm[:, g, :], PC[:, :]),
               reads=[b_PC[0], b_PC[1]], writes=[b_Ytm], tag=f"t{t}.p{p}.Yc")
            for h in range(H):
                def yev(eh, h=h, p=p, g=g):
                    pc = (h % 2) * 512 + (h // 2) * 64
                    return eh.scalar_tensor_tensor(
                        Ytm[:, g, h * 64:(h + 1) * 64], PD[:, pc:pc + 64],
                        sdo_tm[:, p, h:h + 1], Ytm[:, g, h * 64:(h + 1) * 64],
                        ALU.mult, ALU.add)
                st("V", yev, reads=[b_PD[0], b_PD[1], b_Ytm, b_dssdo],
                   writes=[b_Ytm], tag=f"t{t}.p{p}.Yev{h}")

        if t == 0:
            dbg_dump_tm("Y", Ytm, b_Ytm)

        for g in range(NG):
            st("V", lambda eh, g=g: eh.tensor_tensor(Ytm[:, g, :], Ytm[:, g, :],
                                                     zs[:, g, :], ALU.mult),
               reads=[b_Ytm, b_zs], writes=[b_Ytm], tag=f"t{t}.gate{g}")
        for g in range(NG):
            for k in range(8):
                pslot = PA if ((g * 8 + k) % 2 == 0) else PB
                pb = psum_halves[id(pslot)][0]
                st("T", lambda eh, g=g, k=k, pslot=pslot: eh.transpose(
                    pslot[:, 0:128], Ytm[:, g, k * 128:(k + 1) * 128], ident_s[:, :]),
                   reads=[b_Ytm, b_consts], writes=[pb], tag=f"t{t}.yT{g}.{k}")
                st("V", lambda eh, g=g, k=k, pslot=pslot: eh.tensor_copy(
                    Bfm[:, k, g * 128:(g + 1) * 128], pslot[:, 0:128]),
                   reads=[pb], writes=[b_Bfm], tag=f"t{t}.yTe{g}.{k}")

        for q in range(4):
            wslot, bslot = ws[q % 2], [b_wsh[q % 2][0], b_wsh[q % 2][1]]
            st("S", dma(wslot[:, :].rearrange("p (k j) -> p k j", j=256),
                        w_out_h[:, q * 256:(q + 1) * 256].rearrange(
                            "(k p) j -> p k j", p=128)),
               reads=bslot, writes=bslot, dma="dS", tag=f"t{t}.ldwo{q}")
            for g in range(NG):
                pslot = [PA, PA, PB, PB][g]
                pc0 = 512 * (g % 2) + 256 * (q % 2)
                pb = pbufs(pslot, pc0, pc0 + 256)

                def mmwo(eh, g=g, wslot=wslot, pslot=pslot, pc0=pc0):
                    ins = None
                    for k in range(8):
                        ins = eh.matmul(pslot[:, pc0:pc0 + 256],
                                        Bfm[:, k, g * 128:(g + 1) * 128],
                                        wslot[:, k * 256:(k + 1) * 256],
                                        start=(k == 0), stop=(k == 7))
                    return ins
                st("T", mmwo, reads=[b_Bfm] + bslot, writes=pb, tag=f"t{t}.mmwo{q}.{g}")
                st("V", lambda eh, g=g, q=q, pslot=pslot, pc0=pc0: eh.tensor_tensor(
                    xres[:, g, q * 256:(q + 1) * 256],
                    xres[:, g, q * 256:(q + 1) * 256],
                    pslot[:, pc0:pc0 + 256], ALU.add),
                   reads=pb + [b_xres], writes=[b_xres], tag=f"t{t}.evwo{q}.{g}")

        if t == 0:
            dbg_dump_tm("x1", xres, b_xres)

        layernorm(xres, b_xres, buf2, b_buf2, f"t{t}.ln2")
        for g in range(NG):
            emit_quant(buf2[:, g, :], zs[:, g, :], D, [b_buf2], [b_zs], f"t{t}.qx{g}")
        if t == 0:
            dbg_dump_tm("xq", zs, b_zs)
        for g in range(NG):
            for k in range(8):
                pslot = PA if ((g * 8 + k) % 2 == 0) else PB
                pb = psum_halves[id(pslot)][0]
                st("T", lambda eh, g=g, k=k, pslot=pslot: eh.transpose(
                    pslot[:, 0:128], zs[:, g, k * 128:(k + 1) * 128], ident_s[:, :]),
                   reads=[b_zs, b_consts], writes=[pb], tag=f"t{t}.qT{g}.{k}")
                st("V", lambda eh, g=g, k=k, pslot=pslot: eh.tensor_copy(
                    fmA[:, k, g * 128:(g + 1) * 128], pslot[:, 0:128]),
                   reads=[pb], writes=[b_fmA], tag=f"t{t}.qTe{g}.{k}")

        for half in range(2):
            for ic in range(N_IC):
                par = ic % 2
                cof = par * 1024
                st("G", dma(ws[0][:, cof:cof + 1024].rearrange("p (k i) -> p k i", i=IC_W),
                            wqg_h[:, ic * IC_W:(ic + 1) * IC_W].rearrange(
                                "(k p) i -> p k i", p=128)),
                   reads=[b_wsh[0][par], b_wqg], writes=[b_wsh[0][par]], dma="dG",
                   tag=f"t{t}.f{half}.ldg{ic}")
                st("G", dma(ws[1][:, cof:cof + 1024].rearrange("p (k i) -> p k i", i=IC_W),
                            wqu_h[:, ic * IC_W:(ic + 1) * IC_W].rearrange(
                                "(k p) i -> p k i", p=128)),
                   reads=[b_wsh[1][par], b_wqu], writes=[b_wsh[1][par]], dma="dG",
                   tag=f"t{t}.f{half}.ldu{ic}")
                st("G", dma(ws[2][:, cof:cof + 1024],
                            wqd_h[ic * IC_W:(ic + 1) * IC_W, :]),
                   reads=[b_wsh[2][par], b_wqd], writes=[b_wsh[2][par]], dma="dG",
                   tag=f"t{t}.f{half}.ldd{ic}")
                for g2 in range(2):
                    g = half * 2 + g2

                    def mk_mm(wi, pc0, g=g, cof=cof):
                        def fn(eh):
                            ins = None
                            for k in range(8):
                                ins = eh.matmul(PA[:, pc0:pc0 + IC_W],
                                                fmA[:, k, g * 128:(g + 1) * 128],
                                                ws[wi][:, cof + k * IC_W:cof + (k + 1) * IC_W],
                                                start=(k == 0), stop=(k == 7))
                            return ins
                        return fn
                    st("T", mk_mm(0, 0), reads=[b_fmA, b_wsh[0][par]], writes=[b_PA[0]],
                       tag=f"t{t}.f{half}.mmg{ic}.{g2}")
                    st("T", mk_mm(1, 512), reads=[b_fmA, b_wsh[1][par]], writes=[b_PA[1]],
                       tag=f"t{t}.f{half}.mmu{ic}.{g2}")
                    st("A", lambda eh, g2=g2: eh.activation(
                        gate_s[:, g2 * IC_W:(g2 + 1) * IC_W], PA[:, 0:IC_W], AF.Silu),
                       reads=[b_PA[0]], writes=[b_ffn], tag=f"t{t}.f{half}.sg{ic}.{g2}")
                    st("V", lambda eh, g2=g2: eh.tensor_tensor(
                        h_s[:, g2 * IC_W:(g2 + 1) * IC_W],
                        gate_s[:, g2 * IC_W:(g2 + 1) * IC_W],
                        PA[:, 512:512 + IC_W], ALU.mult),
                       reads=[b_ffn, b_PA[1]], writes=[b_ffn], tag=f"t{t}.f{half}.h{ic}.{g2}")
                    emit_quant(h_s[:, g2 * IC_W:(g2 + 1) * IC_W],
                               hq_s[:, g2 * IC_W:(g2 + 1) * IC_W], IC_W,
                               [b_ffn], [b_ffn], f"t{t}.f{half}.qh{ic}.{g2}")
                    st("T", lambda eh, g2=g2: eh.transpose(
                        PB[:, 0:128], hq_s[:, g2 * IC_W:(g2 + 1) * IC_W], ident_s[:, :]),
                       reads=[b_ffn, b_consts], writes=[b_PB[0]],
                       tag=f"t{t}.f{half}.hT{ic}.{g2}")
                    st("V", lambda eh, g2=g2: eh.tensor_copy(
                        hqfm[:, g2, :], PB[:, 0:128]),
                       reads=[b_PB[0]], writes=[b_ffn],
                       tag=f"t{t}.f{half}.hTe{ic}.{g2}")
                    pdown = PC if g2 == 0 else PD
                    pdb = b_PC if g2 == 0 else b_PD
                    for jc in range(2):
                        def mmdown(eh, g2=g2, jc=jc, ic=ic, cof=cof, pdown=pdown):
                            return eh.matmul(
                                pdown[:, jc * 512:(jc + 1) * 512],
                                hqfm[:, g2, :],
                                ws[2][:, cof + jc * 512:cof + (jc + 1) * 512],
                                start=(ic == 0), stop=(ic == N_IC - 1))
                        st("T", mmdown, reads=[b_ffn, b_wsh[2][par]], writes=[pdb[jc]],
                           tag=f"t{t}.f{half}.mmd{ic}.{g2}.{jc}")
            for g2 in range(2):
                g = half * 2 + g2
                pdown = PC if g2 == 0 else PD
                pdb = b_PC if g2 == 0 else b_PD
                for jc in range(2):
                    st("V", lambda eh, g=g, jc=jc, pdown=pdown: eh.tensor_tensor(
                        xres[:, g, jc * 512:(jc + 1) * 512],
                        xres[:, g, jc * 512:(jc + 1) * 512],
                        pdown[:, jc * 512:(jc + 1) * 512], ALU.add),
                       reads=[pdb[jc], b_xres], writes=[b_xres],
                       tag=f"t{t}.f{half}.evd{g2}.{jc}")
        if t == 0:
            dbg_dump_tm("out_t0", xres, b_xres)
        st("S", dma(out_h[tb:tb + TS, :].rearrange("(g p) d -> p g d", p=128),
                    xres[:, :, :]),
           reads=[b_xres], writes=[b_out], dma="dS", tag=f"t{t}.stout")

    if max_steps is not None:
        del ctx.steps[max_steps:]
    ctx.emit()
    for cm in reversed(exits):
        cm.__exit__(None, None, None)
    nc.finalize()
    return nc, sorted(dbg_h.keys())

from concourse.bass_utils import run_bass_kernel_spmd

_CACHED = {}


def kernel(**inputs):
    if "nc" not in _CACHED:
        _CACHED["nc"], _ = build_program(NT=8, dbg=())
    nc = _CACHED["nc"]
    in_maps = [host_prep(inputs, core) for core in range(8)]
    res = run_bass_kernel_spmd(nc, in_maps, core_ids=list(range(8)))
    return np.stack([np.asarray(res.results[i]["out"]) for i in range(8)], axis=0)


# version 08
# speedup vs baseline: 1.3153x; 1.3153x over previous
import numpy as np
import concourse.bass as bass
import concourse.mybir as mybir

F32 = mybir.dt.float32
BF16 = mybir.dt.bfloat16
AF = mybir.ActivationFunctionType
ALU = mybir.AluOpType
AX = mybir.AxisListType

TS = 512
NG = 4
NCH = 8
NPAIR = 4
L = 4096
D = 1024
H = 16
I_FFN = 2816
IC_W = 128
N_IC = I_FFN // IC_W

ENGS = ("T", "V", "A", "G", "S")
DRAIN_ENGS = {"V"}


class Buf:
    __slots__ = ("name", "waw", "writers", "readers")

    def __init__(self, name, waw=True):
        self.name = name
        self.waw = waw
        self.writers = []
        self.readers = []


class Step:
    __slots__ = ("idx", "eng", "fn", "dma", "certify", "deps", "needed_cross",
                 "seq", "tag")

    def __init__(self, idx, eng, fn, dma, certify, tag):
        self.idx = idx
        self.eng = eng
        self.fn = fn
        self.dma = dma
        self.certify = certify
        self.deps = None
        self.needed_cross = False
        self.seq = None
        self.tag = tag


class Ctx:
    def __init__(self, nc):
        self.nc = nc
        self.steps = []

    def buf(self, name, waw=True):
        return Buf(name, waw)

    def step(self, eng, fn, reads=(), writes=(), dma=None, certify="last", tag=""):
        s = Step(len(self.steps), eng, fn, dma, certify, tag)
        deps = {}
        for b in reads:
            for w in b.writers:
                deps[w.idx] = w
        for b in writes:
            for r in b.readers:
                deps[r.idx] = r
            if b.waw:
                for w in b.writers:
                    deps[w.idx] = w
        deps.pop(s.idx, None)
        s.deps = [deps[i] for i in sorted(deps)]
        for b in reads:
            if not b.readers or b.readers[-1] is not s:
                b.readers.append(s)
        for b in writes:
            if b.waw:
                b.writers = [s]
                b.readers = []
            else:
                b.writers.append(s)
        self.steps.append(s)
        return s

    NDMA_LANES = 8

    def emit(self):
        nc = self.nc
        for s in self.steps:
            for d in s.deps:
                if d.dma is None and d.eng != s.eng:
                    d.needed_cross = True
        eng_count = {e: 0 for e in ENGS}
        dma_count = {}
        lane_occ = {}
        for s in self.steps:
            if s.dma is not None:
                n = dma_count.get(s.dma, 0)
                dma_count[s.dma] = n + 1
                lane = n % self.NDMA_LANES
                key = (s.dma, lane)
                lane_occ[key] = lane_occ.get(key, 0) + 1
                s.seq = (key, lane_occ[key])
            elif s.needed_cross:
                eng_count[s.eng] += 1
                s.seq = eng_count[s.eng]
        self.final_lane_counts = dict(lane_occ)

        sems = {}
        sem_cms = []
        for e in ENGS:
            if eng_count[e]:
                cm = nc.semaphore("sem_" + e)
                sems[e] = cm.__enter__()
                sem_cms.append(cm)
        for key in sorted(lane_occ):
            cm = nc.semaphore(f"dma_{key[0]}_{key[1]}")
            sems[key] = cm.__enter__()
            sem_cms.append(cm)

        block_cm = nc.Block()
        block = block_cm.__enter__()
        eng_steps = {e: [s for s in self.steps if s.eng == e] for e in ENGS}

        def make_section(eng_name):
            def section(eh):
                waited = {}
                last_idx = -1
                drained_idx = -1
                drained_idx = -1
                for s in eng_steps[eng_name]:
                    need_drain = False
                    waits = []
                    for d in s.deps:
                        if d.dma is not None:
                            key, occ = d.seq
                            val = 16 * occ
                            if waited.get(key, -1) < val:
                                waits.append((key, val))
                        elif d.eng == eng_name:
                            if eng_name in DRAIN_ENGS and d.idx > drained_idx:
                                need_drain = True
                        else:
                            key, val = d.eng, d.seq
                            assert d.seq is not None, (s.tag, d.tag)
                            if waited.get(key, -1) < val:
                                waits.append((key, val))
                    if s.dma is not None:
                        key, occ = s.seq
                        val = 16 * (occ - 1)
                        if val > 0 and waited.get(key, -1) < val:
                            waits.append((key, val))
                    if need_drain:
                        eh.drain()
                        drained_idx = last_idx
                    for key, val in waits:
                        eh.wait_ge(sems[key], val)
                        waited[key] = val
                    last = s.fn(eh)
                    if s.dma is not None:
                        assert last is not None, s.tag
                        key, occ = s.seq
                        last.then_inc(sems[key], 16)
                    elif s.needed_cross:
                        assert last is not None, s.tag
                        last.then_inc(sems[s.eng], 1)
                    last_idx = s.idx
                if eng_name == "S":
                    for key in sorted(self.final_lane_counts):
                        eh.wait_ge(sems[key], 16 * self.final_lane_counts[key])
            return section

        block.tensor(make_section("T"))
        block.vector(make_section("V"))
        block.scalar(make_section("A"))
        block.gpsimd(make_section("G"))
        block.sync(make_section("S"))
        block_cm.__exit__(None, None, None)
        for cm in reversed(sem_cms):
            cm.__exit__(None, None, None)



def host_prep(inputs, core):
    im = {}
    im["x"] = np.ascontiguousarray(np.asarray(inputs["L0_attn_x"])[core])
    im["w_in_t"] = np.ascontiguousarray(np.asarray(inputs["L0_attn_W_in"]).T)
    im["w_dt_t"] = np.ascontiguousarray(np.asarray(inputs["L0_attn_W_dt"]).T)
    im["w_b_t"] = np.ascontiguousarray(np.asarray(inputs["L0_attn_W_B"]).T)
    im["w_c_t"] = np.ascontiguousarray(np.asarray(inputs["L0_attn_W_C"]).T)
    im["w_out_t"] = np.ascontiguousarray(np.asarray(inputs["L0_attn_W_out"]).T)
    im["a_log"] = np.ascontiguousarray(np.asarray(inputs["L0_attn_A_log"]).reshape(16, 1))
    im["b_dt"] = np.ascontiguousarray(np.asarray(inputs["L0_attn_b_dt"]).reshape(16, 1))
    im["w_gate"] = np.ascontiguousarray(np.asarray(inputs["L0_ffn_w_gate"]))
    im["w_up"] = np.ascontiguousarray(np.asarray(inputs["L0_ffn_w_up"]))
    im["w_down"] = np.ascontiguousarray(np.asarray(inputs["L0_ffn_w_down"]))
    im["ident"] = np.eye(128, dtype=np.float32)
    u = np.arange(64)
    im["cum_in"] = np.tile(np.where(u[:, None] <= u[None, :], 1.0, 0.0), (2, 1)).astype(np.float32)
    im["cum_rev"] = np.tile(np.where(u[:, None] > u[None, :], 1.0, 0.0), (2, 1)).astype(np.float32)
    im["m0"] = np.tile(np.where(u[:, None] <= u[None, :], 0.0, -1e30), (2, 1)).astype(np.float32)
    im["ones_row"] = np.ones((1, 128), np.float32)
    return im


DBG_SHAPES = {
    "xn1": [TS, D], "dt_hm": [16, TS], "acs_hm": [16, TS], "Bfm": [8, 128, TS],
    "Cfm": [8, 128, TS], "X": [TS, D], "E0": [128, 1024], "Y": [TS, D],
    "x1": [TS, D], "xq": [TS, D], "wqg": [D, I_FFN], "out_t0": [TS, D],
    "zs": [TS, D],
}


def build_program(NT=8, dbg=(), max_steps=None):
    nc = bass.Bass("TRN2", target_bir_lowering=False, debug=False,
                   detect_race_conditions=False)
    ctx = Ctx(nc)

    dI = lambda name, shape: nc.dram_tensor(name, shape, F32, kind="ExternalInput").ap()
    x_h = dI("x", [L, D])
    w_in_h = dI("w_in_t", [D, 2 * D])
    w_dt_h = dI("w_dt_t", [D, H])
    w_b_h = dI("w_b_t", [D, D])
    w_c_h = dI("w_c_t", [D, D])
    w_out_h = dI("w_out_t", [D, D])
    a_log_h = dI("a_log", [H, 1])
    b_dt_h = dI("b_dt", [H, 1])
    w_gate_h = dI("w_gate", [I_FFN, D])
    w_up_h = dI("w_up", [I_FFN, D])
    w_down_h = dI("w_down", [D, I_FFN])
    ident_h = dI("ident", [128, 128])
    cum_in_h = dI("cum_in", [128, 64])
    cum_rev_h = dI("cum_rev", [128, 64])
    m0_h = dI("m0", [128, 64])
    ones_row_h = dI("ones_row", [1, 128])

    out_h = nc.dram_tensor("out", [L, D], F32, kind="ExternalOutput").ap()
    wqg_h = nc.dram_tensor("wqg", [D, I_FFN], F32, kind="Internal").ap()
    wqu_h = nc.dram_tensor("wqu", [D, I_FFN], F32, kind="Internal").ap()
    wqd_h = nc.dram_tensor("wqd", [I_FFN, D], F32, kind="Internal").ap()

    dbg = set(dbg)
    dbg_h = {}

    def dbg_out(name):
        if name in dbg:
            dbg_h[name] = nc.dram_tensor("dbg_" + name, DBG_SHAPES[name], F32,
                                         kind="ExternalOutput").ap()
            return dbg_h[name]
        return None

    exits = []

    def sb(name, shape, dtype=F32):
        cm = nc.sbuf_tensor(name, shape, dtype)
        t = cm.__enter__()
        exits.append(cm)
        return t

    def ps(name, shape):
        cm = nc.psum_tensor(name, shape, F32)
        t = cm.__enter__()
        exits.append(cm)
        return t

    xres = sb("xres", [128, NG, D])
    buf2 = sb("buf2", [128, NG, D])
    fmA = sb("fmA", [128, 8, TS])
    zs = sb("zs", [128, NG, D])
    Bfm = sb("Bfm", [128, 8, TS])
    Cfm = sb("Cfm", [128, 8, TS])
    Ytm = sb("Ytm", [128, NG, D])
    sc1 = sb("sc1", [128, 1024])
    sc2 = sb("sc2", [128, 1024])
    sc3 = sb("sc3", [128, 1024])
    sc4 = sb("sc4", [128, 1024])
    Rst = sb("Rst", [128, 512])
    ws = [sb(f"ws{i}", [128, 4096]) for i in range(2)]

    dt_hm = sb("dt_hm", [16, TS])
    A_hm = sb("A_hm", [16, TS])
    acs_hm = sb("acs_hm", [16, TS])
    asum = sb("asum", [16, NCH])
    lam_hm = sb("lam_hm", [16, NCH])
    acs_pair = sb("acs_pair", [1, 1024])
    lam_flat = sb("lam_flat", [1, H * NCH])
    dt_tm = sb("dt_tm", [128, NG, H])
    A_tm = sb("A_tm", [128, NG, H])
    acs_tm = sb("acs_tm", [128, NPAIR, H])
    ds_tm = sb("ds_tm", [128, NPAIR, H])
    sdo_tm = sb("sdo_tm", [128, NPAIR, H])
    lam_all = sb("lam_all", [128, 8, 8])
    stats = sb("stats", [128, 6, NG])
    w_dt_s = sb("w_dt_s", [128, 8, H])
    ident_s = sb("ident_s", [128, 128])
    cum_in_s = sb("cum_in_s", [128, 64])
    cum_rev_s = sb("cum_rev_s", [128, 64])
    m0_s = sb("m0_s", [128, 64])
    ones_s = sb("ones_s", [1, 128])
    alog_s = sb("alog_s", [16, 1])
    bdt_s = sb("bdt_s", [16, 1])
    negea = sb("negea", [16, 1])
    negea_row = sb("negea_row", [1, 16])
    negea_tile = sb("negea_tile", [128, 16])
    actc = sb("actc", [128, 4])
    zc16 = sb("zc16", [16, 64])
    qm = sb("qm", [128, 176])
    qs = sb("qs", [128, 176])
    qi2 = sb("qi2", [128, 176])
    qsh = sb("qsh", [128, 176])
    qt6 = sb("qt6", [128, 176])
    qa_t = sb("qa_t", [128, D])
    qu_t = sb("qu_t", [128, D])
    qw1 = sb("qw1", [128, D], BF16)
    qw2 = sb("qw2", [128, D], BF16)
    qr1 = sb("qr1", [128, D], BF16)
    qr2 = sb("qr2", [128, D], BF16)
    gate_s = sb("gate_s", [128, 512])
    h_s = sb("h_s", [128, 512])
    hq_s = sb("hq_s", [128, 512])
    hqfm = sb("hqfm", [128, 4, 128])

    PA = ps("PA", [128, 1024])
    PB = ps("PB", [128, 1024])
    PC = ps("PC", [128, 1024])
    PD = ps("PD", [128, 1024])

    B = ctx.buf
    b_xres = B("xres")
    b_buf2 = B("buf2")
    b_fmA = B("fmA")
    b_zs = B("zs")
    b_Bfm = B("Bfm")
    b_Cfm = B("Cfm")
    b_Ytm = B("Ytm")
    b_sc1 = B("sc1")
    b_sc2 = B("sc2")
    b_sc3 = B("sc3")
    b_sc4 = B("sc4")
    b_R = B("Rst")
    b_wsh = [[B(f"ws{i}h{j}") for j in range(2)] for i in range(2)]
    b_dt = B("dt")
    b_acs = B("acs")
    b_lam = B("lam")
    b_acsflat = B("acsflat")
    b_lamflat = B("lamflat")
    b_dttm = B("dttm")
    b_acstm = B("acstm")
    b_dssdo = B("dssdo")
    b_lampair = B("lampair")
    b_stats = B("stats")
    b_consts = B("consts")
    b_q = B("quant")
    b_ffn = B("ffn")
    b_PA = [B("PA0"), B("PA1")]
    b_PB = [B("PB0"), B("PB1")]
    b_PC = [B("PC0"), B("PC1")]
    b_PD = [B("PD0"), B("PD1")]
    b_wqg = B("wqg", waw=False)
    b_wqu = B("wqu", waw=False)
    b_wqd = B("wqd", waw=False)
    b_out = B("out_dram", waw=False)
    b_dbg = B("dbg", waw=False)

    psum_halves = {id(PA): b_PA, id(PB): b_PB, id(PC): b_PC, id(PD): b_PD}

    def pbufs(pt, c0, c1):
        hs = psum_halves[id(pt)]
        out = []
        if c0 < 512:
            out.append(hs[0])
        if c1 > 512:
            out.append(hs[1])
        return out

    st = ctx.step

    def dma(dst_ap, src_ap):
        def fn(eh):
            return eh.dma_start(out=dst_ap, in_=src_ap)
        return fn

    for dst, src in [(ident_s[:, :], ident_h), (cum_in_s[:, :], cum_in_h),
                     (cum_rev_s[:, :], cum_rev_h), (m0_s[:, :], m0_h),
                     (ones_s[:, :], ones_row_h), (alog_s[:, :], a_log_h),
                     (bdt_s[:, :], b_dt_h)]:
        st("S", dma(dst, src), reads=[], writes=[b_consts], dma="dS", tag="ldconst")
    st("S", dma(w_dt_s[:, :, :], w_dt_h.rearrange("(k p) h -> p k h", p=128)),
       reads=[], writes=[b_consts], dma="dS", tag="ldwdt")
    st("V", lambda eh: eh.memset(Rst[:, :], 0.0), reads=[], writes=[b_R], tag="zeroR")
    st("V", lambda eh: eh.memset(actc[:, 0:1], 128.0), reads=[], writes=[b_consts], tag="c128")
    st("V", lambda eh: eh.memset(zc16[:, :], 0.0), reads=[], writes=[b_consts], tag="zc16")
    st("V", lambda eh: eh.memset(actc[:, 1:2], float(np.float32(1e-5))), reads=[], writes=[b_consts], tag="ceps")
    st("V", lambda eh: eh.memset(actc[:, 2:3], 0.5), reads=[], writes=[b_consts], tag="chalf")
    st("V", lambda eh: eh.memset(actc[:, 3:4], -1.0), reads=[], writes=[b_consts], tag="cneg1")
    st("A", lambda eh: eh.activation(negea[:, :], alog_s[:, :], AF.Exp),
       reads=[b_consts], writes=[b_consts], tag="negea_exp")
    st("V", lambda eh: eh.tensor_scalar(negea[:, :], negea[:, :], -1.0, None, ALU.mult),
       reads=[b_consts], writes=[b_consts], tag="negea_neg")
    st("T", lambda eh: eh.transpose(PD[0:1, 0:16], negea[:, :], ident_s[0:16, 0:16]),
       reads=[b_consts], writes=[b_PD[0]], tag="negea_T")
    st("V", lambda eh: eh.tensor_copy(negea_row[:, :], PD[0:1, 0:16]),
       reads=[b_PD[0]], writes=[b_consts], tag="negea_row")
    st("T", lambda eh: eh.matmul(PD[0:128, 16:32], ones_s[0:1, :], negea_row[0:1, :],
                                 start=True, stop=True),
       reads=[b_consts], writes=[b_PD[0]], tag="negea_bc")
    st("V", lambda eh: eh.tensor_copy(negea_tile[:, :], PD[0:128, 16:32]),
       reads=[b_PD[0]], writes=[b_consts], tag="negea_tile")

    def emit_quant(src_ap, dst_ap, W, rbufs, wbufs, tag):
        nblk = W // 16
        a3 = src_ap.rearrange("p (g b) -> p g b", b=16)

        def bc(tile):
            return tile[:, 0:nblk].unsqueeze(2).broadcast_to([128, nblk, 16])

        st("V", lambda eh: eh.tensor_reduce(qm[:, 0:nblk], a3, AX.X, ALU.max,
                                            apply_absolute_value=True),
           reads=list(rbufs), writes=[b_q], tag=tag + ".m")
        st("V", lambda eh: eh.tensor_scalar(qs[:, 0:nblk], qm[:, 0:nblk],
                                            float(np.float32(1.0 / 6.0)), float(np.float32(1e-12)),
                                            ALU.mult, ALU.max),
           reads=[b_q], writes=[b_q], tag=tag + ".s")
        st("V", lambda eh: eh.reciprocal(qi2[:, 0:nblk], qs[:, 0:nblk]),
           reads=[b_q], writes=[b_q], tag=tag + ".r")
        st("V", lambda eh: eh.tensor_scalar(qi2[:, 0:nblk], qi2[:, 0:nblk], 2.0, None,
                                            ALU.mult),
           reads=[b_q], writes=[b_q], tag=tag + ".r2")
        st("V", lambda eh: eh.tensor_scalar(qsh[:, 0:nblk], qs[:, 0:nblk], 0.5, None,
                                            ALU.mult),
           reads=[b_q], writes=[b_q], tag=tag + ".sh")
        st("V", lambda eh: eh.tensor_scalar(qt6[:, 0:nblk], qs[:, 0:nblk], 5.0, None,
                                            ALU.mult),
           reads=[b_q], writes=[b_q], tag=tag + ".t6")
        st("A", lambda eh: eh.activation(qa_t[:, 0:W], src_ap, AF.Abs),
           reads=list(rbufs), writes=[b_q], tag=tag + ".abs")
        st("V", lambda eh: eh.tensor_tensor(
            qu_t[:, 0:W].rearrange("p (g b) -> p g b", b=16),
            qa_t[:, 0:W].rearrange("p (g b) -> p g b", b=16), bc(qi2), ALU.mult),
           reads=[b_q], writes=[b_q], tag=tag + ".u")
        st("A", lambda eh: eh.activation(qw1[:, 0:W], qu_t[:, 0:W], AF.Identity,
                                         bias=actc[:, 0:1]),
           reads=[b_q, b_consts], writes=[b_q], tag=tag + ".w1")
        st("A", lambda eh: eh.activation(qw2[:, 0:W], qu_t[:, 0:W], AF.Identity,
                                         bias=actc[:, 0:1], scale=actc[:, 2:3]),
           reads=[b_q, b_consts], writes=[b_q], tag=tag + ".w2")
        st("V", lambda eh: eh.tensor_scalar(qr1[:, 0:W], qw1[:, 0:W], 132.0, 128.0,
                                            ALU.min, ALU.subtract),
           reads=[b_q], writes=[b_q], tag=tag + ".qa")
        st("V", lambda eh: eh.tensor_scalar(qr2[:, 0:W], qw2[:, 0:W], 132.0, 130.0,
                                            ALU.min, ALU.subtract),
           reads=[b_q], writes=[b_q], tag=tag + ".qb")
        st("V", lambda eh: eh.tensor_tensor(
            qu_t[:, 0:W].rearrange("p (g b) -> p g b", b=16),
            qa_t[:, 0:W].rearrange("p (g b) -> p g b", b=16), bc(qt6), ALU.is_gt),
           reads=[b_q], writes=[b_q], tag=tag + ".c6")
        st("V", lambda eh: eh.tensor_scalar(qr2[:, 0:W], qr2[:, 0:W], 0.0, 2.0,
                                            ALU.max, ALU.mult),
           reads=[b_q], writes=[b_q], tag=tag + ".qb2")
        st("V", lambda eh: eh.tensor_tensor(qr2[:, 0:W], qr2[:, 0:W], qr1[:, 0:W],
                                            ALU.add),
           reads=[b_q], writes=[b_q], tag=tag + ".t8")
        st("V", lambda eh: eh.scalar_tensor_tensor(qr2[:, 0:W], qu_t[:, 0:W], 4.0,
                                                   qr2[:, 0:W], ALU.mult, ALU.add),
           reads=[b_q], writes=[b_q], tag=tag + ".q2")
        st("A", lambda eh: eh.activation(qa_t[:, 0:W], src_ap, AF.Sign),
           reads=list(rbufs) + [b_q], writes=[b_q], tag=tag + ".sg")
        st("V", lambda eh: eh.tensor_tensor(
            qa_t[:, 0:W].rearrange("p (g b) -> p g b", b=16),
            qa_t[:, 0:W].rearrange("p (g b) -> p g b", b=16), bc(qsh), ALU.mult),
           reads=[b_q], writes=[b_q], tag=tag + ".sgs")
        st("V", lambda eh: eh.tensor_tensor(dst_ap, qr2[:, 0:W], qa_t[:, 0:W], ALU.mult),
           reads=[b_q], writes=list(wbufs), tag=tag + ".xq")

    def prep_wq(w_h, wq_h, nrows, ncols, b_wq, tag):
        ntile = nrows // 128
        nkb = ncols // 128
        stview = Ytm[:, :, :].rearrange("p a b -> p (a b)")
        for it in range(ntile):
            wsl, bsl = ws[it % 2], b_wsh[it % 2]
            st("G", dma(wsl[:, 0:ncols], w_h[it * 128:(it + 1) * 128, :]),
               reads=bsl, writes=bsl, dma="dG", tag=f"{tag}.ld{it}")
            for c0 in range(0, ncols, 1024):
                w = min(1024, ncols - c0)
                emit_quant(wsl[:, c0:c0 + w], wsl[:, c0:c0 + w], w, bsl, bsl,
                           f"{tag}.q{it}.{c0}")
            for kb in range(nkb):
                pslot = PB if (kb % 2 == 0) else PC
                pb = psum_halves[id(pslot)][0]
                st("T", lambda eh, wsl=wsl, kb=kb, pslot=pslot: eh.transpose(
                    pslot[:, 0:128], wsl[:, kb * 128:(kb + 1) * 128], ident_s[:, :]),
                   reads=bsl + [b_consts], writes=[pb], tag=f"{tag}.T{it}.{kb}")
                st("V", lambda eh, kb=kb, pslot=pslot: eh.tensor_copy(
                    stview[:, (kb % 8) * 128:(kb % 8) * 128 + 128], pslot[:, 0:128]),
                   reads=[pb], writes=[b_Ytm], tag=f"{tag}.evT{it}.{kb}")
                if kb % 8 == 7 or kb == nkb - 1:
                    kb0 = (kb // 8) * 8
                    nkb_i = kb - kb0 + 1
                    st("G", dma(wq_h[kb0 * 128:kb0 * 128 + nkb_i * 128,
                                     it * 128:(it + 1) * 128].rearrange(
                                    "(kb p) t -> p kb t", p=128),
                                stview[:, 0:nkb_i * 128].rearrange(
                                    "p (kb t) -> p kb t", t=128)),
                       reads=[b_Ytm], writes=[b_wq], dma="dG",
                       tag=f"{tag}.st{it}.{kb0}")

    prep_wq(w_gate_h, wqg_h, I_FFN, D, b_wqg, "pg")
    prep_wq(w_up_h, wqu_h, I_FFN, D, b_wqu, "pu")
    prep_wq(w_down_h, wqd_h, D, I_FFN, b_wqd, "pd")

    d = dbg_out("wqg")
    if d is not None:
        for r in range(D // 128):
            st("G", dma(d[r * 128:(r + 1) * 128, :], wqg_h[r * 128:(r + 1) * 128, :]),
               reads=[b_wqg], writes=[b_dbg], dma="dG", tag=f"dbgwqg{r}")

    def dbg_dump_tm(name, buf, rbuf):
        d = dbg_out(name)
        if d is not None:
            st("G", dma(d.rearrange("(g p) d -> p g d", p=128), buf[:, :, :]),
               reads=[rbuf], writes=[b_dbg], dma="dG", tag="dbg" + name)

    for t in range(NT):
        tb = t * TS
        st("S", dma(xres[:, :, :], x_h[tb:tb + TS, :].rearrange("(g p) d -> p g d", p=128)),
           reads=[b_xres], writes=[b_xres], dma="dS", tag=f"t{t}.ldx")

        def layernorm(src_buf, src_b, dst_buf, dst_b, tag):
            for g in range(NG):
                st("V", lambda eh, g=g: eh.reduce_sum(stats[:, 0, g:g + 1],
                                                      src_buf[:, g, :], AX.X),
                   reads=[src_b], writes=[b_stats], tag=f"{tag}.sum{g}")
                st("A", lambda eh, g=g: eh.activation(dst_buf[:, g, :], src_buf[:, g, :],
                                                      AF.Square,
                                                      accum_out=stats[:, 1, g:g + 1]),
                   reads=[src_b], writes=[dst_b, b_stats], tag=f"{tag}.sq{g}")
            st("V", lambda eh: eh.tensor_scalar(stats[:, 2, :], stats[:, 0, :],
                                                float(np.float32(1.0 / D)), None, ALU.mult),
               reads=[b_stats], writes=[b_stats], tag=f"{tag}.mean")
            st("V", lambda eh: eh.tensor_tensor(stats[:, 3, :], stats[:, 2, :],
                                                stats[:, 2, :], ALU.mult),
               reads=[b_stats], writes=[b_stats], tag=f"{tag}.msq")
            st("V", lambda eh: eh.scalar_tensor_tensor(stats[:, 4, :], stats[:, 1, :],
                                                       float(np.float32(1.0 / D)),
                                                       stats[:, 3, :],
                                                       ALU.mult, ALU.subtract),
               reads=[b_stats], writes=[b_stats], tag=f"{tag}.var")
            st("A", lambda eh: eh.activation(stats[:, 5, :], stats[:, 4, :], AF.Sqrt,
                                             bias=actc[:, 1:2]),
               reads=[b_stats, b_consts], writes=[b_stats], tag=f"{tag}.sd")
            st("V", lambda eh: eh.reciprocal(stats[:, 5, :], stats[:, 5, :]),
               reads=[b_stats], writes=[b_stats], tag=f"{tag}.rstd")
            for g in range(NG):
                st("V", lambda eh, g=g: eh.scalar_tensor_tensor(
                    dst_buf[:, g, :], src_buf[:, g, :], stats[:, 2, g:g + 1],
                    stats[:, 5, g:g + 1].broadcast_to([128, D]),
                    ALU.subtract, ALU.mult),
                   reads=[src_b, b_stats], writes=[dst_b], tag=f"{tag}.norm{g}")

        layernorm(xres, b_xres, buf2, b_buf2, f"t{t}.ln1")
        if t == 0:
            dbg_dump_tm("xn1", buf2, b_buf2)

        for g in range(NG):
            for k in range(8):
                pslot = PA if ((g * 8 + k) % 2 == 0) else PB
                pb = psum_halves[id(pslot)][0]
                st("T", lambda eh, g=g, k=k, pslot=pslot: eh.transpose(
                    pslot[:, 0:128], buf2[:, g, k * 128:(k + 1) * 128], ident_s[:, :]),
                   reads=[b_buf2, b_consts], writes=[pb], tag=f"t{t}.xT{g}.{k}")
                st("V", lambda eh, g=g, k=k, pslot=pslot: eh.tensor_copy(
                    fmA[:, k, g * 128:(g + 1) * 128], pslot[:, 0:128]),
                   reads=[pb], writes=[b_fmA], tag=f"t{t}.xTe{g}.{k}")

        def mm_dt(eh):
            ins = None
            for k in range(8):
                ins = eh.matmul(PD[0:16, 0:TS], w_dt_s[:, k, :], fmA[:, k, 0:TS],
                                start=(k == 0), stop=(k == 7))
            return ins
        st("T", mm_dt, reads=[b_consts, b_fmA], writes=[b_PD[0]], tag=f"t{t}.mmdt")
        st("A", lambda eh: eh.activation(dt_hm[:, :], PD[0:16, 0:TS], AF.Identity,
                                         bias=bdt_s[:, 0:1]),
           reads=[b_PD[0], b_consts], writes=[b_dt], tag=f"t{t}.dtz")
        st("A", lambda eh: eh.activation(A_hm[:, :], dt_hm[:, :], AF.Abs),
           reads=[b_dt], writes=[b_dt], tag=f"t{t}.dtabs")
        st("A", lambda eh: eh.activation(A_hm[:, :], A_hm[:, :], AF.Exp,
                                         scale=actc[0:16, 3:4]),
           reads=[b_dt, b_consts], writes=[b_dt], tag=f"t{t}.dtexp")
        st("A", lambda eh: eh.activation(A_hm[:, :], A_hm[:, :], AF.Ln, bias=1.0),
           reads=[b_dt], writes=[b_dt], tag=f"t{t}.dtln")
        st("V", lambda eh: eh.scalar_tensor_tensor(dt_hm[:, :], dt_hm[:, :], 0.0,
                                                   A_hm[:, :], ALU.max, ALU.add),
           reads=[b_dt], writes=[b_dt], tag=f"t{t}.dtsp")
        if t == 0:
            d = dbg_out("dt_hm")
            if d is not None:
                st("G", dma(d, dt_hm[:, :]), reads=[b_dt], writes=[b_dbg], dma="dG",
                   tag="dbgdt")
        st("V", lambda eh: eh.tensor_scalar(A_hm[:, :], dt_hm[:, :], negea[:, 0:1],
                                            None, ALU.mult),
           reads=[b_dt, b_consts], writes=[b_dt], tag=f"t{t}.ahm")
        st("V", lambda eh: eh.reduce_sum(asum[:, :],
                                         A_hm[:, :].rearrange("p (c l) -> p c l", l=64),
                                         AX.X),
           reads=[b_dt], writes=[b_lam], tag=f"t{t}.asum")
        st("A", lambda eh: eh.activation(lam_hm[:, :], asum[:, :], AF.Exp),
           reads=[b_lam], writes=[b_lam], tag=f"t{t}.lam")
        for c in range(NCH):
            st("V", lambda eh, c=c: eh.tensor_tensor_scan(
                acs_hm[:, c * 64:(c + 1) * 64], zc16[:, :],
                A_hm[:, c * 64:(c + 1) * 64], 0.0, ALU.add, ALU.add),
               reads=[b_dt, b_consts], writes=[b_acs], tag=f"t{t}.scan{c}")
        if t == 0:
            d = dbg_out("acs_hm")
            if d is not None:
                st("G", dma(d, acs_hm[:, :]), reads=[b_acs], writes=[b_dbg], dma="dG",
                   tag="dbgacs")
        st("G", dma(lam_flat[0:1, :].rearrange("o (h c) -> o h c", h=16),
                    lam_hm[:, :]),
           reads=[b_lam], writes=[b_lamflat], dma="dG", tag=f"t{t}.lamflat")
        st("T", lambda eh: eh.matmul(
            PD[0:64, 544:608], ones_s[0:1, 0:64],
            lam_flat[0:1, :].rearrange("o (j two c) -> o two c j", two=2, c=8)[0:1, 0, :, :],
            start=True, stop=True),
           reads=[b_lamflat, b_consts], writes=[b_PD[1]], tag=f"t{t}.lamA")
        st("T", lambda eh: eh.matmul(
            PD[64:128, 544:608], ones_s[0:1, 0:64],
            lam_flat[0:1, :].rearrange("o (j two c) -> o two c j", two=2, c=8)[0:1, 1, :, :],
            start=True, stop=True, tile_position=(0, 64)),
           reads=[b_lamflat, b_consts], writes=[b_PD[1]], tag=f"t{t}.lamB")
        st("V", lambda eh: eh.tensor_copy(lam_all[:, :, :].rearrange("p a b -> p (a b)"),
                                          PD[:, 544:608]),
           reads=[b_PD[1]], writes=[b_lampair], tag=f"t{t}.lamE")

        for g in range(NG):
            st("T", lambda eh, g=g: eh.transpose(PD[0:128, 512:528],
                                                 dt_hm[:, g * 128:(g + 1) * 128],
                                                 ident_s[0:16, 0:16]),
               reads=[b_dt, b_consts], writes=[b_PD[1]], tag=f"t{t}.dtT{g}")
            st("V", lambda eh, g=g: eh.tensor_copy(dt_tm[:, g, :], PD[0:128, 512:528]),
               reads=[b_PD[1]], writes=[b_dttm], tag=f"t{t}.dtTe{g}")
            st("V", lambda eh, g=g: eh.tensor_tensor(A_tm[:, g, :], dt_tm[:, g, :],
                                                     negea_tile[:, :], ALU.mult),
               reads=[b_dttm, b_consts], writes=[b_dttm], tag=f"t{t}.atm{g}")

        for c in range(NCH):
            g, par = c // 2, c % 2
            r0 = par * 64
            p = c // 2
            st("T", lambda eh, g=g, r0=r0: eh.matmul(
                PD[r0:r0 + 64, 512:528], cum_in_s[r0:r0 + 64, :],
                A_tm[r0:r0 + 64, g, :], start=True, stop=True,
                tile_position=(r0, r0)),
               reads=[b_dttm, b_consts], writes=[b_PD[1]], tag=f"t{t}.acsmm{c}")
            st("T", lambda eh, g=g, r0=r0: eh.matmul(
                PD[r0:r0 + 64, 528:544], cum_rev_s[r0:r0 + 64, :],
                A_tm[r0:r0 + 64, g, :], start=True, stop=True,
                tile_position=(r0, r0)),
               reads=[b_dttm, b_consts], writes=[b_PD[1]], tag=f"t{t}.rcmm{c}")
            st("V", lambda eh, p=p, r0=r0: eh.tensor_copy(
                acs_tm[r0:r0 + 64, p, :], PD[r0:r0 + 64, 512:528]),
               reads=[b_PD[1]], writes=[b_acstm], tag=f"t{t}.acstm{c}")
            st("A", lambda eh, p=p, r0=r0: eh.activation(
                sdo_tm[r0:r0 + 64, p, :], PD[r0:r0 + 64, 512:528], AF.Exp),
               reads=[b_PD[1]], writes=[b_dssdo], tag=f"t{t}.sdo{c}")
            st("A", lambda eh, p=p, r0=r0: eh.activation(
                ds_tm[r0:r0 + 64, p, :], PD[r0:r0 + 64, 528:544], AF.Exp),
               reads=[b_PD[1]], writes=[b_dssdo], tag=f"t{t}.ds{c}")

        for jc in range(4):
            wslot, bslot = ws[jc % 2], [b_wsh[jc % 2][0], b_wsh[jc % 2][1]]
            st("S", dma(wslot[:, :].rearrange("p (k j) -> p k j", j=512),
                        w_in_h[:, jc * 512:(jc + 1) * 512].rearrange(
                            "(k p) j -> p k j", p=128)),
               reads=bslot, writes=bslot, dma="dS", tag=f"t{t}.ldwin{jc}")
            for g in range(NG):
                pslot = [PA, PA, PB, PB][g]
                pc0 = 512 * (g % 2)
                pb = pbufs(pslot, pc0, pc0 + 512)

                def mmxz(eh, g=g, wslot=wslot, pslot=pslot, pc0=pc0):
                    ins = None
                    for k in range(8):
                        ins = eh.matmul(pslot[:, pc0:pc0 + 512],
                                        fmA[:, k, g * 128:(g + 1) * 128],
                                        wslot[:, k * 512:(k + 1) * 512],
                                        start=(k == 0), stop=(k == 7))
                    return ins
                st("T", mmxz, reads=[b_fmA] + bslot, writes=pb, tag=f"t{t}.mmxz{jc}.{g}")
                if jc < 2:
                    st("V", lambda eh, g=g, jc=jc, pslot=pslot, pc0=pc0: eh.tensor_copy(
                        buf2[:, g, jc * 512:(jc + 1) * 512], pslot[:, pc0:pc0 + 512]),
                       reads=pb, writes=[b_buf2], tag=f"t{t}.evxin{jc}.{g}")
                else:
                    st("A", lambda eh, g=g, jc=jc, pslot=pslot, pc0=pc0: eh.activation(
                        zs[:, g, (jc - 2) * 512:(jc - 1) * 512],
                        pslot[:, pc0:pc0 + 512], AF.Silu),
                       reads=pb, writes=[b_zs], tag=f"t{t}.evz{jc}.{g}")

        for g in range(NG):
            st("V", lambda eh, g=g: eh.tensor_tensor(
                buf2[:, g, :].rearrange("p (h f) -> p h f", f=64),
                buf2[:, g, :].rearrange("p (h f) -> p h f", f=64),
                dt_tm[:, g, :].unsqueeze(2).broadcast_to([128, H, 64]), ALU.mult),
               reads=[b_buf2, b_dttm], writes=[b_buf2], tag=f"t{t}.X{g}")
        if t == 0:
            dbg_dump_tm("X", buf2, b_buf2)

        for (w_hh, dstbuf, dstb, nm) in [(w_b_h, Bfm, b_Bfm, "B"), (w_c_h, Cfm, b_Cfm, "C")]:
            for q in range(2):
                wslot, bslot = ws[q % 2], [b_wsh[q % 2][0], b_wsh[q % 2][1]]
                st("S", dma(wslot[:, :].rearrange("p (k j) -> p k j", j=512),
                            w_hh[:, q * 512:(q + 1) * 512].rearrange(
                                "(k p) j -> p k j", p=128)),
                   reads=bslot, writes=bslot, dma="dS", tag=f"t{t}.ldw{nm}{q}")
                for j2 in range(4):
                    nk = q * 4 + j2
                    pslot = [PA, PB, PA, PB][j2]
                    pc0 = 512 * (j2 // 2)
                    pb = pbufs(pslot, pc0, pc0 + 512)

                    def mmbc(eh, wslot=wslot, j2=j2, pslot=pslot, pc0=pc0):
                        ins = None
                        for k in range(8):
                            ins = eh.matmul(
                                pslot[:, pc0:pc0 + 512],
                                wslot[:, k * 512 + j2 * 128:k * 512 + j2 * 128 + 128],
                                fmA[:, k, 0:TS], start=(k == 0), stop=(k == 7))
                        return ins
                    st("T", mmbc, reads=[b_fmA] + bslot, writes=pb, tag=f"t{t}.mm{nm}{nk}")
                    st("V", lambda eh, nk=nk, pslot=pslot, pc0=pc0, dstbuf=dstbuf:
                       eh.tensor_copy(dstbuf[:, nk, :], pslot[:, pc0:pc0 + 512]),
                       reads=pb, writes=[dstb], tag=f"t{t}.ev{nm}{nk}")

        if t == 0:
            for nm, bb, br in [("Bfm", Bfm, b_Bfm), ("Cfm", Cfm, b_Cfm)]:
                d = dbg_out(nm)
                if d is not None:
                    st("G", dma(d.rearrange("k p tt -> p k tt"), bb[:, :, :]),
                       reads=[br], writes=[b_dbg], dma="dG", tag="dbg" + nm)
            dbg_dump_tm("zs", zs, b_zs)

        for p in range(NPAIR):
            g = p
            for half64 in range(2):
                c = 2 * p + half64
                st("G", dma(acs_pair[0:1, 0:1024].rearrange("o (h l) -> o h l", l=64)
                            if half64 < 0 else
                            acs_pair[0:1, :].rearrange("o (h l) -> o h l", l=64),
                            acs_hm[:, c * 64:(c + 1) * 64]),
                   reads=[b_acs, b_acsflat], writes=[b_acsflat], dma="dG",
                   tag=f"t{t}.p{p}.acsst{half64}")
                r0 = half64 * 64
                for par in range(2):
                    def mmD(eh, r0=r0, par=par):
                        rhs = acs_pair[0:1, :].rearrange(
                            "o (j par l) -> o par j l", par=2, l=64)[0:1, par, :, :]
                        return eh.matmul(PA[r0:r0 + 64, par * 512:(par + 1) * 512],
                                         ones_s[0:1, 0:64], rhs, start=True, stop=True,
                                         tile_position=(0, r0))
                    st("T", mmD, reads=[b_acsflat, b_consts],
                       writes=pbufs(PA, par * 512, (par + 1) * 512),
                       tag=f"t{t}.p{p}.D{half64}.{par}")
            for par in range(2):
                st("V", lambda eh, p=p, par=par: eh.tensor_tensor(
                    sc1[:, par * 512:(par + 1) * 512].rearrange("q (j l) -> q j l", l=64),
                    PA[:, par * 512:(par + 1) * 512].rearrange("q (j l) -> q j l", l=64),
                    acs_tm[:, p, :].rearrange("q (j par2) -> q par2 j", par2=2)[
                        :, par, :].unsqueeze(2).broadcast_to([128, 8, 64]), ALU.subtract),
                   reads=[b_PA[par], b_acstm], writes=[b_sc1], tag=f"t{t}.p{p}.Dsub{par}")
            st("V", lambda eh: eh.tensor_tensor(
                sc1[:, :].rearrange("q (h l) -> q h l", l=64),
                sc1[:, :].rearrange("q (h l) -> q h l", l=64),
                m0_s[:, :].unsqueeze(1).broadcast_to([128, H, 64]), ALU.add),
               reads=[b_sc1, b_consts], writes=[b_sc1], tag=f"t{t}.p{p}.Dm0")
            st("A", lambda eh: eh.activation(sc1[:, :], sc1[:, :], AF.Exp),
               reads=[b_sc1], writes=[b_sc1], tag=f"t{t}.p{p}.Eexp")
            if p == 0 and t == 0:
                d = dbg_out("E0")
                if d is not None:
                    st("G", dma(d, sc1[:, :]), reads=[b_sc1], writes=[b_dbg],
                       dma="dG", tag="dbgE0")

            for half64 in range(2):
                c = 2 * p + half64
                r0 = half64 * 64
                for h in range(H):
                    hb = (h % 2) * 64
                    pc = (h % 2) * 512 + (h // 2) * 64

                    def mmGT(eh, c=c, r0=r0, h=h, hb=hb, pc=pc):
                        return eh.matmul(
                            PB[r0:r0 + 64, pc:pc + 64],
                            Bfm[hb:hb + 64, h // 2, c * 64:(c + 1) * 64],
                            Cfm[hb:hb + 64, h // 2, c * 64:(c + 1) * 64],
                            start=True, stop=True, tile_position=(hb, r0))
                    st("T", mmGT, reads=[b_Bfm, b_Cfm],
                       writes=pbufs(PB, pc, pc + 64),
                       tag=f"t{t}.p{p}.GT{c}.{h}")
            st("V", lambda eh: eh.tensor_tensor(sc2[:, :], sc1[:, :], PB[:, :], ALU.mult),
               reads=[b_sc1, b_PB[0], b_PB[1]], writes=[b_sc2], tag=f"t{t}.p{p}.EG")

            for nk in range(8):
                st("T", lambda eh, nk=nk, p=p: eh.transpose(
                    PC[:, 0:128], Bfm[:, nk, p * 128:(p + 1) * 128], ident_s[:, :]),
                   reads=[b_Bfm, b_consts], writes=[b_PC[0]], tag=f"t{t}.p{p}.BT{nk}")
                st("V", lambda eh, nk=nk: eh.tensor_copy(
                    sc4[:, nk * 128:(nk + 1) * 128], PC[:, 0:128]),
                   reads=[b_PC[0]], writes=[b_sc4], tag=f"t{t}.p{p}.BTe{nk}")
            st("V", lambda eh, p=p: eh.tensor_tensor(
                sc3[:, :].rearrange("q (h f) -> q h f", f=64),
                sc4[:, :].rearrange("q (h f) -> q h f", f=64),
                ds_tm[:, p, :].unsqueeze(2).broadcast_to([128, H, 64]), ALU.mult),
               reads=[b_sc4, b_dssdo], writes=[b_sc3], tag=f"t{t}.p{p}.btil")

            for half64 in range(2):
                r0 = half64 * 64
                for h in range(H):
                    hb = (h % 2) * 64

                    def mmS(eh, r0=r0, h=h, hb=hb, g=g, half64=half64):
                        return eh.matmul(
                            PA[hb:hb + 64,
                               half64 * 512 + (h // 2) * 64:half64 * 512 + (h // 2) * 64 + 64],
                            sc3[r0:r0 + 64, h * 64:(h + 1) * 64],
                            buf2[r0:r0 + 64, g, h * 64:(h + 1) * 64],
                            start=True, stop=True, tile_position=(r0, hb))
                    st("T", mmS, reads=[b_sc3, b_buf2],
                       writes=pbufs(PA, half64 * 512, half64 * 512 + 512),
                       tag=f"t{t}.p{p}.S{half64}.{h}")

            for half64 in range(2):
                r0 = half64 * 64
                for h in range(H):
                    pc = (h % 2) * 512 + (h // 2) * 64

                    def mmYd(eh, r0=r0, h=h, g=g, pc=pc):
                        return eh.matmul(
                            PC[r0:r0 + 64, h * 64:(h + 1) * 64],
                            sc2[r0:r0 + 64, pc:pc + 64],
                            buf2[r0:r0 + 64, g, h * 64:(h + 1) * 64],
                            start=True, stop=True, tile_position=(r0, r0))
                    st("T", mmYd, reads=[b_sc2, b_buf2],
                       writes=pbufs(PC, h * 64, (h + 1) * 64),
                       tag=f"t{t}.p{p}.Yd{half64}.{h}")

            for half64 in range(2):
                c = 2 * p + half64
                r0 = half64 * 64
                for h in range(H):
                    hb = (h % 2) * 64
                    pc = (h % 2) * 512 + (h // 2) * 64

                    def mmYo(eh, c=c, r0=r0, h=h, hb=hb, pc=pc):
                        return eh.matmul(
                            PD[r0:r0 + 64, pc:pc + 64],
                            Cfm[hb:hb + 64, h // 2, c * 64:(c + 1) * 64],
                            Rst[hb:hb + 64, (h // 2) * 64:(h // 2) * 64 + 64],
                            start=True, stop=True, tile_position=(hb, r0))
                    st("T", mmYo, reads=[b_Cfm, b_R],
                       writes=pbufs(PD, pc, pc + 64),
                       tag=f"t{t}.p{p}.Yo{c}.{h}")
                cc = 2 * p + half64
                for j in range(8):
                    def rupd(eh, j=j, half64=half64, cc=cc):
                        return eh.scalar_tensor_tensor(
                            Rst[:, j * 64:(j + 1) * 64], Rst[:, j * 64:(j + 1) * 64],
                            lam_all[:, cc, j:j + 1],
                            PA[:, half64 * 512 + j * 64:half64 * 512 + j * 64 + 64],
                            ALU.mult, ALU.add)
                    st("V", rupd, reads=[b_R, b_lampair, b_PA[half64]], writes=[b_R],
                       tag=f"t{t}.p{p}.Rup{half64}.{j}")

            st("V", lambda eh, g=g: eh.tensor_copy(Ytm[:, g, :], PC[:, :]),
               reads=[b_PC[0], b_PC[1]], writes=[b_Ytm], tag=f"t{t}.p{p}.Yc")
            for h in range(H):
                def yev(eh, h=h, p=p, g=g):
                    pc = (h % 2) * 512 + (h // 2) * 64
                    return eh.scalar_tensor_tensor(
                        Ytm[:, g, h * 64:(h + 1) * 64], PD[:, pc:pc + 64],
                        sdo_tm[:, p, h:h + 1], Ytm[:, g, h * 64:(h + 1) * 64],
                        ALU.mult, ALU.add)
                st("V", yev, reads=[b_PD[0], b_PD[1], b_Ytm, b_dssdo],
                   writes=[b_Ytm], tag=f"t{t}.p{p}.Yev{h}")

        if t == 0:
            dbg_dump_tm("Y", Ytm, b_Ytm)

        for g in range(NG):
            st("V", lambda eh, g=g: eh.tensor_tensor(Ytm[:, g, :], Ytm[:, g, :],
                                                     zs[:, g, :], ALU.mult),
               reads=[b_Ytm, b_zs], writes=[b_Ytm], tag=f"t{t}.gate{g}")
        for g in range(NG):
            for k in range(8):
                pslot = PA if ((g * 8 + k) % 2 == 0) else PB
                pb = psum_halves[id(pslot)][0]
                st("T", lambda eh, g=g, k=k, pslot=pslot: eh.transpose(
                    pslot[:, 0:128], Ytm[:, g, k * 128:(k + 1) * 128], ident_s[:, :]),
                   reads=[b_Ytm, b_consts], writes=[pb], tag=f"t{t}.yT{g}.{k}")
                st("V", lambda eh, g=g, k=k, pslot=pslot: eh.tensor_copy(
                    Bfm[:, k, g * 128:(g + 1) * 128], pslot[:, 0:128]),
                   reads=[pb], writes=[b_Bfm], tag=f"t{t}.yTe{g}.{k}")

        for q in range(2):
            wslot, bslot = ws[q % 2], [b_wsh[q % 2][0], b_wsh[q % 2][1]]
            st("S", dma(wslot[:, :].rearrange("p (k j) -> p k j", j=512),
                        w_out_h[:, q * 512:(q + 1) * 512].rearrange(
                            "(k p) j -> p k j", p=128)),
               reads=bslot, writes=bslot, dma="dS", tag=f"t{t}.ldwo{q}")
            for g in range(NG):
                pslot = [PA, PA, PB, PB][g]
                pc0 = 512 * (g % 2)
                pb = pbufs(pslot, pc0, pc0 + 512)

                def mmwo(eh, g=g, wslot=wslot, pslot=pslot, pc0=pc0):
                    ins = None
                    for k in range(8):
                        ins = eh.matmul(pslot[:, pc0:pc0 + 512],
                                        Bfm[:, k, g * 128:(g + 1) * 128],
                                        wslot[:, k * 512:(k + 1) * 512],
                                        start=(k == 0), stop=(k == 7))
                    return ins
                st("T", mmwo, reads=[b_Bfm] + bslot, writes=pb, tag=f"t{t}.mmwo{q}.{g}")
                st("V", lambda eh, g=g, q=q, pslot=pslot, pc0=pc0: eh.tensor_tensor(
                    xres[:, g, q * 512:(q + 1) * 512],
                    xres[:, g, q * 512:(q + 1) * 512],
                    pslot[:, pc0:pc0 + 512], ALU.add),
                   reads=pb + [b_xres], writes=[b_xres], tag=f"t{t}.evwo{q}.{g}")

        if t == 0:
            dbg_dump_tm("x1", xres, b_xres)

        layernorm(xres, b_xres, buf2, b_buf2, f"t{t}.ln2")
        for g in range(NG):
            emit_quant(buf2[:, g, :], zs[:, g, :], D, [b_buf2], [b_zs], f"t{t}.qx{g}")
        if t == 0:
            dbg_dump_tm("xq", zs, b_zs)
        for g in range(NG):
            for k in range(8):
                pslot = PA if ((g * 8 + k) % 2 == 0) else PB
                pb = psum_halves[id(pslot)][0]
                st("T", lambda eh, g=g, k=k, pslot=pslot: eh.transpose(
                    pslot[:, 0:128], zs[:, g, k * 128:(k + 1) * 128], ident_s[:, :]),
                   reads=[b_zs, b_consts], writes=[pb], tag=f"t{t}.qT{g}.{k}")
                st("V", lambda eh, g=g, k=k, pslot=pslot: eh.tensor_copy(
                    fmA[:, k, g * 128:(g + 1) * 128], pslot[:, 0:128]),
                   reads=[pb], writes=[b_fmA], tag=f"t{t}.qTe{g}.{k}")

        IC_LIST = [(0, 512), (512, 512), (1024, 512), (1536, 512), (2048, 512),
                   (2560, 256)]
        gview = [Bfm[:, :, :],
                 buf2[:, :, :].rearrange("p a b -> p (a b)").rearrange(
                     "p (k j) -> p k j", j=512)]
        uview = [Cfm[:, :, :],
                 zs[:, :, :].rearrange("p a b -> p (a b)").rearrange(
                     "p (k j) -> p k j", j=512)]
        dview = [Ytm[:, :, :],
                 ws[0][:, :].rearrange("p (k j) -> p k j", j=1024)]
        gbufs = [[b_Bfm], [b_buf2]]
        ubufs = [[b_Cfm], [b_zs]]
        dbufs = [[b_Ytm], [b_wsh[0][0], b_wsh[0][1]]]
        for half in range(2):
            for ic, (i0, w) in enumerate(IC_LIST):
                par = ic % 2
                nkk = w // 128
                st("G", dma(gview[par][:, :, 0:w],
                            wqg_h[:, i0:i0 + w].rearrange("(k p) i -> p k i", p=128)),
                   reads=gbufs[par] + [b_wqg], writes=gbufs[par], dma="dG",
                   tag=f"t{t}.f{half}.ldg{ic}")
                st("G", dma(uview[par][:, :, 0:w],
                            wqu_h[:, i0:i0 + w].rearrange("(k p) i -> p k i", p=128)),
                   reads=ubufs[par] + [b_wqu], writes=ubufs[par], dma="dG",
                   tag=f"t{t}.f{half}.ldu{ic}")
                st("S", dma(dview[par][:, 0:nkk, :],
                            wqd_h[i0:i0 + w, :].rearrange("(kk p) j -> p kk j", p=128)),
                   reads=dbufs[par] + [b_wqd], writes=dbufs[par], dma="dS",
                   tag=f"t{t}.f{half}.ldd{ic}")
                for g2 in range(2):
                    g = half * 2 + g2

                    def mk_mm(vw, pc0, g=g, w=w):
                        def fn(eh):
                            ins = None
                            for k in range(8):
                                ins = eh.matmul(PA[:, pc0:pc0 + w],
                                                fmA[:, k, g * 128:(g + 1) * 128],
                                                vw[:, k, 0:w],
                                                start=(k == 0), stop=(k == 7))
                            return ins
                        return fn
                    st("T", mk_mm(gview[par], 0), reads=[b_fmA] + gbufs[par],
                       writes=[b_PA[0]], tag=f"t{t}.f{half}.mmg{ic}.{g2}")
                    st("T", mk_mm(uview[par], 512), reads=[b_fmA] + ubufs[par],
                       writes=[b_PA[1]], tag=f"t{t}.f{half}.mmu{ic}.{g2}")
                    st("A", lambda eh, w=w: eh.activation(
                        gate_s[:, 0:w], PA[:, 0:w], AF.Silu),
                       reads=[b_PA[0]], writes=[b_ffn], tag=f"t{t}.f{half}.sg{ic}.{g2}")
                    st("V", lambda eh, w=w: eh.tensor_tensor(
                        h_s[:, 0:w], gate_s[:, 0:w], PA[:, 512:512 + w], ALU.mult),
                       reads=[b_ffn, b_PA[1]], writes=[b_ffn], tag=f"t{t}.f{half}.h{ic}.{g2}")
                    emit_quant(h_s[:, 0:w], hq_s[:, 0:w], w,
                               [b_ffn], [b_ffn], f"t{t}.f{half}.qh{ic}.{g2}")
                    for kk in range(nkk):
                        st("T", lambda eh, kk=kk: eh.transpose(
                            PB[:, 0:128], hq_s[:, kk * 128:(kk + 1) * 128], ident_s[:, :]),
                           reads=[b_ffn, b_consts], writes=[b_PB[0]],
                           tag=f"t{t}.f{half}.hT{ic}.{g2}.{kk}")
                        st("V", lambda eh, kk=kk: eh.tensor_copy(
                            hqfm[:, kk, :], PB[:, 0:128]),
                           reads=[b_PB[0]], writes=[b_ffn],
                           tag=f"t{t}.f{half}.hTe{ic}.{g2}.{kk}")
                    pdown = PC if g2 == 0 else PD
                    pdb = b_PC if g2 == 0 else b_PD
                    for jc in range(2):
                        def mmdown(eh, g2=g2, jc=jc, ic=ic, par=par, nkk=nkk,
                                   pdown=pdown):
                            ins = None
                            for kk in range(nkk):
                                ins = eh.matmul(
                                    pdown[:, jc * 512:(jc + 1) * 512],
                                    hqfm[:, kk, :],
                                    dview[par][:, kk, jc * 512:(jc + 1) * 512],
                                    start=(ic == 0 and kk == 0),
                                    stop=(ic == len(IC_LIST) - 1 and kk == nkk - 1))
                            return ins
                        st("T", mmdown, reads=[b_ffn] + dbufs[par], writes=[pdb[jc]],
                           tag=f"t{t}.f{half}.mmd{ic}.{g2}.{jc}")
            for g2 in range(2):
                g = half * 2 + g2
                pdown = PC if g2 == 0 else PD
                pdb = b_PC if g2 == 0 else b_PD
                for jc in range(2):
                    st("V", lambda eh, g=g, jc=jc, pdown=pdown: eh.tensor_tensor(
                        xres[:, g, jc * 512:(jc + 1) * 512],
                        xres[:, g, jc * 512:(jc + 1) * 512],
                        pdown[:, jc * 512:(jc + 1) * 512], ALU.add),
                       reads=[pdb[jc], b_xres], writes=[b_xres],
                       tag=f"t{t}.f{half}.evd{g2}.{jc}")
        if t == 0:
            dbg_dump_tm("out_t0", xres, b_xres)
        st("S", dma(out_h[tb:tb + TS, :].rearrange("(g p) d -> p g d", p=128),
                    xres[:, :, :]),
           reads=[b_xres], writes=[b_out], dma="dS", tag=f"t{t}.stout")

    if max_steps is not None:
        del ctx.steps[max_steps:]
    ctx.emit()
    for cm in reversed(exits):
        cm.__exit__(None, None, None)
    nc.finalize()
    return nc, sorted(dbg_h.keys())

from concourse.bass_utils import run_bass_kernel_spmd

_CACHED = {}


def kernel(**inputs):
    if "nc" not in _CACHED:
        _CACHED["nc"], _ = build_program(NT=8, dbg=())
    nc = _CACHED["nc"]
    in_maps = [host_prep(inputs, core) for core in range(8)]
    res = run_bass_kernel_spmd(nc, in_maps, core_ids=list(range(8)))
    return np.stack([np.asarray(res.results[i]["out"]) for i in range(8)], axis=0)
